# version 15 (final)
# baseline (speedup 1.0000x reference)
import numpy as np
import ml_dtypes
from contextlib import ExitStack

import concourse.bass as bass
import concourse.mybir as mybir
from concourse.bass_utils import run_bass_kernel_spmd

F32 = mybir.dt.float32
BF16 = mybir.dt.bfloat16
AX = mybir.AxisListType.X
OP = mybir.AluOpType
AF = mybir.ActivationFunctionType
BF = ml_dtypes.bfloat16


class Tok:
    __slots__ = ("sem", "key", "val", "needed", "ival")

    def __init__(self, sem, key, val, needed=True):
        self.sem = sem
        self.key = key
        self.val = val
        self.needed = needed
        self.ival = val


class Buf:
    __slots__ = ("name", "last_w", "readers")

    def __init__(self, name):
        self.name = name
        self.last_w = None
        self.readers = {}


class Chan:

    def __init__(self, prog, queue, name):
        self.prog = prog
        self.queue = queue
        self.sem = prog.es.enter_context(prog.nc.semaphore(name))
        self.key = ("c", name)
        self.cnt = 0

    def dma(self, out_ap, in_ap, reads=(), writes=(), transpose=False, **kw):
        p = self.prog
        deps = p._deps(reads, writes)
        p._resolve(self.queue, deps)
        self.cnt += 1
        v = 16 * self.cnt
        sem = self.sem
        if transpose:
            p.cmds[self.queue].append(
                lambda eng, o=out_ap, i=in_ap, s=sem:
                eng.dma_start_transpose(o, i).then_inc(s, 16))
        else:
            p.cmds[self.queue].append(
                lambda eng, o=out_ap, i=in_ap, s=sem, kw=dict(kw):
                eng.dma_start(o, i, **kw).then_inc(s, 16))
        t = Tok(sem, self.key, v)
        p._note(t, reads, writes)
        return t

    def batch_token(self):
        return Tok(self.sem, self.key, 16 * self.cnt)


class Prog:
    ENGINES = ("tensor", "vector", "scalar", "sync", "gpsimd")

    def __init__(self, nc, es):
        self.nc = nc
        self.es = es
        self.cmds = {e: [] for e in self.ENGINES}
        self.sem = {}
        self.cnt = {}
        self.toks = {}
        self.last_tok = {}
        for e in ("tensor", "vector", "scalar"):
            self.sem[e] = es.enter_context(nc.semaphore("sem_" + e))
            self.cnt[e] = 0
            self.toks[e] = []
            self.last_tok[e] = None
        self.wm = {e: {} for e in self.ENGINES}

    def _deps(self, reads, writes):
        deps = []
        for b in reads:
            if b.last_w is not None:
                deps.append(b.last_w)
        for b in writes:
            if b.last_w is not None:
                deps.append(b.last_w)
            deps.extend(b.readers.values())
        return deps

    def _resolve(self, eng, deps):
        wm = self.wm[eng]
        own = ("e", eng)
        for t in deps:
            if t.key == own and eng == "tensor":
                continue
            if wm.get(t.key, 0) >= t.val:
                continue
            wm[t.key] = t.val
            t.needed = True
            self.cmds[eng].append(
                lambda E, t=t: E.wait_ge(t.sem, t.ival))

    def _note(self, tok, reads, writes):
        for b in reads:
            r = b.readers.get(tok.key)
            if r is None or r.val < tok.val:
                b.readers[tok.key] = tok
        for b in writes:
            b.last_w = tok
            b.readers = {}

    def op(self, eng, fn, reads=(), writes=(), deps=()):
        dl = self._deps(reads, writes)
        dl.extend(deps)
        self._resolve(eng, dl)
        self.cnt[eng] += 1
        sem = self.sem[eng]
        v = self.cnt[eng]
        t = Tok(sem, ("e", eng), v, needed=False)
        self.toks[eng].append(t)
        self.last_tok[eng] = t
        self.cmds[eng].append(
            lambda E, f=fn, t=t: f(E).then_inc(t.sem, 1) if t.needed else f(E))
        self._note(t, reads, writes)
        return t

    def barrier(self, chans):
        toks = []
        for e in ("tensor", "vector", "scalar"):
            if self.last_tok[e] is not None:
                toks.append(self.last_tok[e])
        for ch in chans:
            if ch.cnt:
                toks.append(ch.batch_token())
        for e in self.ENGINES:
            self._resolve(e, toks)

    def finalize(self, end_toks):
        for t in end_toks:
            self.cmds["sync"].append(
                lambda E, s=t.sem, v=t.val: E.wait_ge(s, v))
        for e, toks in self.toks.items():
            n = 0
            for t in toks:
                if t.needed:
                    n += 1
                t.ival = n
        with self.nc.Block() as block:
            @block.tensor
            def _(eng):
                for f in self.cmds["tensor"]:
                    f(eng)

            @block.vector
            def _(eng):
                for f in self.cmds["vector"]:
                    f(eng)

            @block.scalar
            def _(eng):
                for f in self.cmds["scalar"]:
                    f(eng)

            @block.sync
            def _(eng):
                for f in self.cmds["sync"]:
                    f(eng)

            @block.gpsimd
            def _(eng):
                for f in self.cmds["gpsimd"]:
                    f(eng)


def splits(total, size):
    return [(s, min(size, total - s)) for s in range(0, total, size)]


def _emit_layernorm(prog, x_ap, b_x, sqb, b_sqb, stt, b_stt, out_ap, b_out, D,
                    epsc, b_epsc):
    prog.op("vector", lambda E_: E_.reduce_sum(stt[:, 0:1], x_ap, AX),
            reads=[b_x], writes=[b_stt])
    ncut = (D + 511) // 512
    for ci in range(ncut):
        c0 = ci * 512
        cl = min(512, D - c0)
        acc_col = stt[:, 1:2] if ci == 0 else stt[:, 2:3]
        prog.op("scalar",
                lambda E_, o=sqb[:, 0:cl], i=x_ap[:, c0:c0 + cl]:
                E_.activation(o, i, AF.Square),
                reads=[b_x], writes=[b_sqb])
        prog.op("vector",
                lambda E_, o=acc_col, i=sqb[:, 0:cl]: E_.reduce_sum(o, i, AX),
                reads=[b_sqb], writes=[b_stt])
    if ncut == 2:
        prog.op("vector", lambda E_: E_.tensor_tensor(stt[:, 2:3], stt[:, 1:2],
                                                      stt[:, 2:3], OP.add),
                reads=[b_stt], writes=[b_stt])
    else:
        prog.op("vector", lambda E_: E_.tensor_copy(stt[:, 2:3], stt[:, 1:2]),
                reads=[b_stt], writes=[b_stt])
    inv = 1.0 / D
    prog.op("vector", lambda E_: E_.tensor_scalar(stt[:, 3:4], stt[:, 0:1], inv,
                                                  None, OP.mult),
            reads=[b_stt], writes=[b_stt])
    prog.op("vector", lambda E_: E_.tensor_tensor(stt[:, 4:5], stt[:, 3:4],
                                                  stt[:, 3:4], OP.mult),
            reads=[b_stt], writes=[b_stt])
    prog.op("vector", lambda E_: E_.tensor_scalar(stt[:, 5:6], stt[:, 2:3], inv,
                                                  stt[:, 4:5], OP.mult, OP.subtract),
            reads=[b_stt], writes=[b_stt])
    prog.op("scalar", lambda E_: E_.activation(stt[:, 5:6], stt[:, 5:6], AF.Ln,
                                               bias=epsc[:, 0:1], scale=1.0),
            reads=[b_stt, b_epsc], writes=[b_stt])
    prog.op("scalar", lambda E_: E_.activation(stt[:, 6:7], stt[:, 5:6], AF.Exp,
                                               bias=0.0, scale=-0.5),
            reads=[b_stt], writes=[b_stt])
    prog.op("vector", lambda E_: E_.scalar_tensor_tensor(stt[:, 7:8], stt[:, 3:4],
                                                         -1.0, stt[:, 6:7],
                                                         OP.mult, OP.mult),
            reads=[b_stt], writes=[b_stt])
    prog.op("vector", lambda E_: E_.tensor_scalar(out_ap, x_ap, stt[:, 6:7],
                                                  stt[:, 7:8], OP.mult, OP.add),
            reads=[b_x, b_stt], writes=[b_out])


def build_core_program(cfg):
    S = cfg["S"]
    D = cfg["D"]
    DK = cfg["DK"]
    H = cfg["H"]
    E = cfg["E"]
    TT = cfg["TT"]
    scale = cfg["scale"]
    TOPK = cfg.get("TOPK", 2)
    EPS = 1e-5

    ndc = D // 128
    nkc = DK // 128
    nhc = H // 128
    ntb = S // 512
    nqb = S // 128
    ntile = S // TT
    nj = TT // 128
    ncol = TT // 512 if TT >= 512 else 1
    colw = min(512, TT)
    EE = E + 1
    nsc = S // 512
    ntc = S // 128
    natg = (ntc + 3) // 4
    qb_per_tile = nqb // ntile

    nc = bass.Bass("TRN2", target_bir_lowering=False)

    x_d = nc.dram_tensor("x", [S, D], F32, kind="ExternalInput").ap()
    wq_d = nc.dram_tensor("wq", [D, DK], F32, kind="ExternalInput").ap()
    wk_d = nc.dram_tensor("wk", [D, DK], F32, kind="ExternalInput").ap()
    wv_d = nc.dram_tensor("wv", [D, DK], F32, kind="ExternalInput").ap()
    bqkv_d = nc.dram_tensor("bqkv", [3, DK], F32, kind="ExternalInput").ap()
    gw_d = nc.dram_tensor("gw", [E, D], F32, kind="ExternalInput").ap()
    gb_d = nc.dram_tensor("gb", [1, E], F32, kind="ExternalInput").ap()
    w1_d = nc.dram_tensor("w1s", [EE, D, H], F32, kind="ExternalInput").ap()
    w3_d = nc.dram_tensor("w3s", [EE, D, H], F32, kind="ExternalInput").ap()
    w2_d = nc.dram_tensor("w2s", [EE, H, D], F32, kind="ExternalInput").ap()
    b1_d = nc.dram_tensor("b1s", [EE, H], F32, kind="ExternalInput").ap()
    b3_d = nc.dram_tensor("b3s", [EE, H], F32, kind="ExternalInput").ap()
    b2_d = nc.dram_tensor("b2s", [EE, D], F32, kind="ExternalInput").ap()
    idb_d = nc.dram_tensor("ident_bf", [128, 128], BF16, kind="ExternalInput").ap()
    idf_d = nc.dram_tensor("ident_f", [128, 128], F32, kind="ExternalInput").ap()
    ones_d = nc.dram_tensor("ones1", [1, 128], F32, kind="ExternalInput").ap()

    out_d = nc.dram_tensor("out", [S, D], F32, kind="ExternalOutput").ap()

    q_scr = nc.dram_tensor("q_scr", [ntb, 128, nkc * 512], BF16, kind="Internal").ap()
    x2_scr = nc.dram_tensor("x2_scr", [S, D], F32, kind="Internal").ap()
    x2b_scr = nc.dram_tensor("x2b_scr", [S, D], BF16, kind="Internal").ap()

    es = ExitStack()
    prog = Prog(nc, es)

    def sb(name, shape, dt, scope=es):
        return scope.enter_context(nc.sbuf_tensor(name, shape, dt))

    def ps(name, shape, dt, scope=es):
        return scope.enter_context(nc.psum_tensor(name, shape, dt))

    ch_const = Chan(prog, "sync", "ch_const")
    ch_qs = [Chan(prog, "sync", f"ch_qs{i}") for i in range(2)]
    ch_q = [Chan(prog, "sync", f"ch_q{i}") for i in range(2)]
    ch_xr = [Chan(prog, "sync", f"ch_xr{i}") for i in range(2)]
    ch_x2s = [Chan(prog, "sync", f"ch_x2s{i}") for i in range(2)]
    ch_y = Chan(prog, "sync", "ch_y")
    ch_xf = [Chan(prog, "sync", f"ch_xf{i}") for i in range(2)]
    ch_out = [Chan(prog, "sync", f"ch_out{i}") for i in range(2)]
    ch_xin = Chan(prog, "gpsimd", "ch_xin")
    ch_x2b = [Chan(prog, "gpsimd", f"ch_x2b{i}") for i in range(2)]
    ch_w13 = [Chan(prog, "gpsimd", f"ch_w13{i}") for i in range(2)]
    ch_w2 = [Chan(prog, "gpsimd", f"ch_w2{i}") for i in range(2)]
    ch_wqkv = Chan(prog, "gpsimd", "ch_wqkv")
    all_chans = ([ch_const, ch_y, ch_xin, ch_wqkv]
                 + ch_qs + ch_q + ch_xr + ch_x2s + ch_out + ch_x2b
                 + ch_w13 + ch_w2 + ch_xf)

    b_qscr = Buf("q_scr")
    b_x2scr = [Buf(f"x2_scr{t}") for t in range(ntile)]
    b_x2bscr = [Buf(f"x2b_scr{t}") for t in range(ntile)]

    idb = sb("idb", [128, 128], BF16)
    gwt = sb("gwt", [128, ndc * E], BF16)
    gbrep = sb("gbrep", [128, E], F32)
    b1c = sb("b1c", [128, EE * nhc], F32)
    b3c = sb("b3c", [128, EE * nhc], F32)
    b2all = sb("b2all", [EE, D], BF16)
    epsc = sb("epsc", [128, 1], F32)
    b_idb, b_idf, b_ones = Buf("idb"), Buf("idf"), Buf("ones1")
    b_gwt, b_gbrep = Buf("gwt"), Buf("gbrep")
    b_b1c, b_b3c, b_b2all = Buf("b1c"), Buf("b3c"), Buf("b2all")
    b_epsc = Buf("epsc")

    pA = ExitStack()
    k_sb = sb("k_sb", [128, nkc * S], BF16, pA)
    v_sb = sb("v_sb", [128, ntc * DK], BF16, pA)
    b_ksb, b_vsb = Buf("k_sb"), Buf("v_sb")

    pro_small = ExitStack()
    idf = sb("idf", [128, 128], F32, pro_small)
    ones1 = sb("ones_s", [1, 128], F32, pro_small)
    bqc = sb("bqc", [128, nkc], F32, pro_small)
    bkc = sb("bkc", [128, nkc], F32, pro_small)
    bvrep = sb("bvrep", [128, DK], BF16, pro_small)
    b_bqc, b_bkc, b_bvrep = Buf("bqc"), Buf("bkc"), Buf("bvrep")

    pro = ExitStack()
    gw_sb = sb("gw_sb", [E, D], F32, pro)
    b1stage = sb("b1stage", [EE, H], F32, pro)
    b3stage = sb("b3stage", [EE, H], F32, pro)
    bq_sb = sb("bq_sb", [1, DK], F32, pro)
    bk_sb = sb("bk_sb", [1, DK], F32, pro)
    bv_sb = sb("bv_sb", [1, DK], F32, pro)
    gb_sb = sb("gb_sb", [1, E], F32, pro)
    b_gw, b_b1s, b_b3s = Buf("gw_sb"), Buf("b1stage"), Buf("b3stage")
    b_bqkv, b_gb = Buf("bqkv"), Buf("gb_sb")

    prog.op("vector", lambda E_: E_.memset(epsc[:], float(EPS)), writes=[b_epsc])
    ch_const.dma(idb[:], idb_d[:], writes=[b_idb])
    ch_const.dma(idf[:], idf_d[:], writes=[b_idf])
    ch_const.dma(ones1[:], ones_d[:], writes=[b_ones])
    ch_const.dma(gw_sb[:], gw_d[:], writes=[b_gw])
    ch_const.dma(b1stage[:], b1_d[:], writes=[b_b1s])
    ch_const.dma(b3stage[:], b3_d[:], writes=[b_b3s])
    ch_const.dma(bq_sb[:], bqkv_d[0:1, :], writes=[b_bqkv])
    ch_const.dma(bk_sb[:], bqkv_d[1:2, :], writes=[b_bqkv])
    ch_const.dma(bv_sb[:], bqkv_d[2:3, :], writes=[b_bqkv])
    ch_const.dma(gb_sb[:], gb_d[:], writes=[b_gb])
    tok_const = ch_const.batch_token()
    for b in (b_idb, b_idf, b_ones, b_gw, b_b1s, b_b3s, b_bqkv, b_gb):
        b.last_w = tok_const

    ch_wqkv.dma(b2all[:], b2_d[:], writes=[b_b2all])

    pps = ExitStack()
    pf = [ps(f"pro_f{i}", [128, 512], F32, pps) for i in range(2)]
    b_pf = [Buf(f"pro_f{i}") for i in range(2)]

    for dc in range(ndc):
        bank = dc % 2
        prog.op("tensor",
                lambda E_, o=pf[bank][0:128, 0:E],
                i=gw_sb[0:E, dc * 128:(dc + 1) * 128], ii=idf[0:E, 0:E]:
                E_.transpose(o, i, ii),
                reads=[b_gw, b_idf], writes=[b_pf[bank]])
        prog.op("vector",
                lambda E_, o=gwt[:, dc * E:(dc + 1) * E], i=pf[bank][0:128, 0:E]:
                E_.tensor_copy(o, i),
                reads=[b_pf[bank]], writes=[b_gwt])
    prog.op("tensor",
            lambda E_: E_.matmul(pf[0][0:128, 0:E], ones1[0:1, :], gb_sb[0:1, :],
                                 start=True, stop=True),
            reads=[b_ones, b_gb], writes=[b_pf[0]])
    prog.op("vector", lambda E_: E_.tensor_copy(gbrep[:], pf[0][0:128, 0:E]),
            reads=[b_pf[0]], writes=[b_gbrep])

    for (stage, bstage, dest, bdest) in ((b1stage, b_b1s, b1c, b_b1c),
                                         (b3stage, b_b3s, b3c, b_b3c)):
        for hc in range(nhc):
            bank = hc % 2
            prog.op("tensor",
                    lambda E_, o=pf[bank][0:128, 0:EE],
                    i=stage[0:EE, hc * 128:(hc + 1) * 128], ii=idf[0:EE, 0:EE]:
                    E_.transpose(o, i, ii),
                    reads=[bstage, b_idf], writes=[b_pf[bank]])
            prog.op("vector",
                    lambda E_,
                    o=dest[:].rearrange("p (e c) -> p e c", e=EE)[:, :, hc],
                    i=pf[bank][0:128, 0:EE]: E_.tensor_copy(o, i),
                    reads=[b_pf[bank]], writes=[bdest])

    for (bsrc, dest, bdest) in ((bq_sb, bqc, b_bqc), (bk_sb, bkc, b_bkc)):
        for kc in range(nkc):
            bank = kc % 2
            prog.op("tensor",
                    lambda E_, o=pf[bank][0:128, 0:1],
                    i=bsrc[0:1, kc * 128:(kc + 1) * 128], ii=idf[0:1, 0:1]:
                    E_.transpose(o, i, ii),
                    reads=[b_bqkv, b_idf], writes=[b_pf[bank]])
            prog.op("vector",
                    lambda E_, o=dest[:, kc:kc + 1], i=pf[bank][0:128, 0:1]:
                    E_.tensor_copy(o, i),
                    reads=[b_pf[bank]], writes=[bdest])
    prog.op("vector",
            lambda E_: E_.tensor_scalar(bqc[:], bqc[:], float(scale), None, OP.mult),
            reads=[b_bqc], writes=[b_bqc])

    for (c0, cl) in splits(DK, 512):
        bank = (c0 // 512) % 2
        prog.op("tensor",
                lambda E_, o=pf[bank][0:128, 0:cl], i=bv_sb[0:1, c0:c0 + cl]:
                E_.matmul(o, ones1[0:1, :], i, start=True, stop=True),
                reads=[b_ones, b_bqkv], writes=[b_pf[bank]])
        prog.op("vector",
                lambda E_, o=bvrep[:, c0:c0 + cl], i=pf[bank][0:128, 0:cl]:
                E_.tensor_copy(o, i),
                reads=[b_pf[bank]], writes=[b_bvrep])

    pps.close()
    prog.barrier(all_chans)
    pro.close()

    p1 = ExitStack()
    wk_sb = sb("wk_sb", [128, ndc * DK], BF16, p1)
    wv_sb = sb("wv_sb", [128, ndc * DK], BF16, p1)
    wq_sb = sb("wq_sb", [128, ndc * DK], BF16, p1)
    xblk = sb("xblk", [128, 4 * D], BF16, p1)
    xfm = sb("xfm", [128, ndc * 512], BF16, p1)
    qstage = [sb(f"qstage{i}", [128, 512], BF16, p1) for i in range(2)]
    b_wk, b_wv, b_wq = Buf("wk"), Buf("wv"), Buf("wq")
    b_xblk, b_xfm = Buf("xblk"), Buf("xfm")
    b_qstage = [Buf(f"qstage{i}") for i in range(2)]

    for (w_d_, w_sb_, b_w_) in ((wk_d, wk_sb, b_wk), (wv_d, wv_sb, b_wv),
                                (wq_d, wq_sb, b_wq)):
        ch_wqkv.dma(w_sb_[:].rearrange("p (c k) -> p c k", c=ndc),
                    w_d_.rearrange("(c p) k -> p c k", p=128),
                    writes=[b_w_])
    tok_wqkv = ch_wqkv.batch_token()
    for b in (b_b2all, b_wk, b_wv, b_wq):
        b.last_w = tok_wqkv

    p1ps = ExitStack()
    f32b = [ps(f"p1f{i}", [128, 512], F32, p1ps) for i in range(3)]
    b_f32b = [Buf(f"p1f{i}") for i in range(3)]
    bf16b = [ps(f"p1b{i}", [128, 1024], BF16, p1ps) for i in range(2)]
    b_bf16b = [Buf(f"p1b{i}") for i in range(2)]

    for blk in range(ntb):
        t0 = blk * 512
        ch_xin.dma(xblk[:].rearrange("p (c d) -> p c d", c=4),
                   x_d[t0:t0 + 512, :].rearrange("(c p) d -> p c d", p=128),
                   writes=[b_xblk])
        for dc in range(ndc):
            bank = dc % 2
            for tc in range(4):
                prog.op("tensor",
                        lambda E_, o=bf16b[bank][:, tc * 128:(tc + 1) * 128],
                        i=xblk[:, tc * D + dc * 128: tc * D + (dc + 1) * 128]:
                        E_.transpose(o, i, idb[:]),
                        reads=[b_xblk, b_idb], writes=[b_bf16b[bank]])
            prog.op("vector",
                    lambda E_, o=xfm[:, dc * 512:(dc + 1) * 512],
                    i=bf16b[bank][:, 0:512]: E_.tensor_copy(o, i),
                    reads=[b_bf16b[bank]], writes=[b_xfm])

        for kc in range(nkc):
            bank = kc % 3
            for dc in range(ndc):
                prog.op("tensor",
                        lambda E_, o=f32b[bank][:],
                        l=wk_sb[:, dc * DK + kc * 128: dc * DK + (kc + 1) * 128],
                        r=xfm[:, dc * 512:(dc + 1) * 512],
                        st=(dc == 0), sp=(dc == ndc - 1):
                        E_.matmul(o, l, r, start=st, stop=sp),
                        reads=[b_wk, b_xfm], writes=[b_f32b[bank]])
            prog.op("vector",
                    lambda E_, o=k_sb[:, kc * S + t0: kc * S + t0 + 512],
                    i=f32b[bank][:], bb=bkc[:, kc:kc + 1]:
                    E_.tensor_scalar(o, i, bb, None, OP.add),
                    reads=[b_f32b[bank], b_bkc], writes=[b_ksb])

        for tc in range(4):
            for (v0, vl) in splits(DK, 512):
                bank = (tc * 2 + v0 // 512) % 3
                for dc in range(ndc):
                    prog.op("tensor",
                            lambda E_, o=f32b[bank][:, 0:vl],
                            l=xfm[:, dc * 512 + tc * 128: dc * 512 + (tc + 1) * 128],
                            r=wv_sb[:, dc * DK + v0: dc * DK + v0 + vl],
                            st=(dc == 0), sp=(dc == ndc - 1):
                            E_.matmul(o, l, r, start=st, stop=sp),
                            reads=[b_wv, b_xfm], writes=[b_f32b[bank]])
                tch = blk * 4 + tc
                prog.op("vector",
                        lambda E_, o=v_sb[:, tch * DK + v0: tch * DK + v0 + vl],
                        i=f32b[bank][:, 0:vl], bb=bvrep[:, v0:v0 + vl]:
                        E_.tensor_tensor(o, i, bb, OP.add),
                        reads=[b_f32b[bank], b_bvrep], writes=[b_vsb])

        for kc in range(nkc):
            bank = kc % 3
            slot = kc % 2
            for dc in range(ndc):
                prog.op("tensor",
                        lambda E_, o=f32b[bank][:],
                        l=wq_sb[:, dc * DK + kc * 128: dc * DK + (kc + 1) * 128],
                        r=xfm[:, dc * 512:(dc + 1) * 512],
                        st=(dc == 0), sp=(dc == ndc - 1):
                        E_.matmul(o, l, r, start=st, stop=sp),
                        reads=[b_wq, b_xfm], writes=[b_f32b[bank]])
            prog.op("vector",
                    lambda E_, o=qstage[slot][:], i=f32b[bank][:],
                    bb=bqc[:, kc:kc + 1]:
                    E_.tensor_scalar(o, i, float(scale), bb, OP.mult, OP.add),
                    reads=[b_f32b[bank], b_bqc], writes=[b_qstage[slot]])
            ch_qs[slot].dma(q_scr[blk, :, kc * 512:(kc + 1) * 512], qstage[slot][:],
                            reads=[b_qstage[slot]], writes=[b_qscr])

    p1ps.close()
    p1.close()
    pro_small.close()
    prog.barrier(all_chans)

    p2 = ExitStack()
    s_sb = [sb(f"s_sb{i}", [128, S], BF16, p2) for i in range(2)]
    a_sb = sb("a_sb", [128, S], BF16, p2)
    at_sb = sb("at_sb", [128, S], BF16, p2)
    qblk = [sb(f"qblk{i}", [128, nkc * 512], BF16, p2) for i in range(2)]
    xres = [sb(f"xres{i}", [128, D], F32, p2) for i in range(2)]
    x2pre = sb("x2pre", [128, D], F32, p2)
    sqtmp = sb("sqtmp", [128, min(512, D)], F32, p2)
    x2stage = [sb(f"x2stage{i}", [128, D], F32, p2) for i in range(2)]
    stats = [sb(f"stats{i}", [128, 16], F32, p2) for i in range(2)]
    st2 = [sb(f"st2_{i}", [128, 8], F32, p2) for i in range(2)]
    lnst = sb("lnst", [128, 8], F32, p2)
    b_ssb = [Buf(f"s_sb{i}") for i in range(2)]
    b_asb, b_atsb = Buf("a_sb"), Buf("at_sb")
    b_qblk = [Buf(f"qblk{i}") for i in range(2)]
    b_xres = [Buf(f"xres{i}") for i in range(2)]
    b_x2pre, b_sqtmp = Buf("x2pre"), Buf("sqtmp")
    b_x2stage = [Buf(f"x2stage{i}") for i in range(2)]
    b_stats = [Buf(f"stats{i}") for i in range(2)]
    b_st2 = [Buf(f"st2_{i}") for i in range(2)]
    b_lnst = Buf("lnst")

    p2ps = ExitStack()
    sbank = [ps(f"p2s{i}", [128, 512], F32, p2ps) for i in range(4)]
    b_sbank = [Buf(f"p2s{i}") for i in range(4)]
    atbank = [ps(f"p2a{i}", [128, 1024], BF16, p2ps) for i in range(2)]
    b_atbank = [Buf(f"p2a{i}") for i in range(2)]
    obank = [ps(f"p2o{i}", [128, 512], F32, p2ps) for i in range(2)]
    b_obank = [Buf(f"p2o{i}") for i in range(2)]

    def p2_load_qgroup(g):
        qslot = g % 2
        ch_q[qslot].dma(qblk[qslot][:], q_scr[g, :, :],
                        reads=[b_qscr], writes=[b_qblk[qslot]])

    def p2_loads(qb):
        slot = qb % 2
        q0 = qb * 128
        ch_xr[slot].dma(xres[slot][:], x_d[q0:q0 + 128, :],
                        writes=[b_xres[slot]])

    def p2_s_stats(qb):
        slot = qb % 2
        par = qb % 2
        qslot = (qb // 4) % 2
        qoff = (qb % 4) * 128
        for sc in range(nsc):
            bank = sc % 4
            for dc in range(nkc):
                prog.op("tensor",
                        lambda E_, o=sbank[bank][:],
                        l=qblk[qslot][:, dc * 512 + qoff: dc * 512 + qoff + 128],
                        r=k_sb[:, dc * S + sc * 512: dc * S + sc * 512 + 512],
                        st=(dc == 0), sp=(dc == nkc - 1):
                        E_.matmul(o, l, r, start=st, stop=sp),
                        reads=[b_qblk[qslot], b_ksb], writes=[b_sbank[bank]])
            prog.op("vector",
                    lambda E_, o=stats[par][:, sc:sc + 1], i=sbank[bank][:]:
                    E_.reduce_max(o, i, AX, negate=True),
                    reads=[b_sbank[bank]], writes=[b_stats[par]])
            prog.op("vector",
                    lambda E_, o=s_sb[par][:, sc * 512:(sc + 1) * 512],
                    i=sbank[bank][:]: E_.tensor_copy(o, i),
                    reads=[b_sbank[bank]], writes=[b_ssb[par]])
        prog.op("vector",
                lambda E_, o=st2[par][:, 0:1], i=stats[par][:, 0:nsc]:
                E_.tensor_reduce(o, i, AX, OP.min),
                reads=[b_stats[par]], writes=[b_st2[par]])

    def p2_attend(qb):
        slot = qb % 2
        par = qb % 2
        q0 = qb * 128
        prog.op("scalar",
                lambda E_: E_.activation(a_sb[:], s_sb[par][:], AF.Exp,
                                         bias=st2[par][:, 0:1], scale=1.0),
                reads=[b_ssb[par], b_st2[par]], writes=[b_asb])
        for g in range(natg):
            bank = g % 2
            n_in = min(4, ntc - g * 4)
            for j in range(n_in):
                tc = g * 4 + j
                prog.op("tensor",
                        lambda E_, o=atbank[bank][:, j * 128:(j + 1) * 128],
                        i=a_sb[:, tc * 128:(tc + 1) * 128]:
                        E_.transpose(o, i, idb[:]),
                        reads=[b_asb, b_idb], writes=[b_atbank[bank]])
            prog.op("vector",
                    lambda E_, o=at_sb[:, g * 512: g * 512 + n_in * 128],
                    i=atbank[bank][:, 0:n_in * 128]: E_.tensor_copy(o, i),
                    reads=[b_atbank[bank]], writes=[b_atsb])
        prog.op("vector",
                lambda E_, o=st2[par][:, 1:2]: E_.reduce_sum(o, a_sb[:], AX),
                reads=[b_asb], writes=[b_st2[par]])
        prog.op("vector",
                lambda E_, o=st2[par][:, 2:3], i=st2[par][:, 1:2]:
                E_.reciprocal(o, i),
                reads=[b_st2[par]], writes=[b_st2[par]])
        for (v0, vl) in splits(DK, 512):
            ob = v0 // 512
            for tc in range(ntc):
                prog.op("tensor",
                        lambda E_, o=obank[ob][:, 0:vl],
                        l=at_sb[:, tc * 128:(tc + 1) * 128],
                        r=v_sb[:, tc * DK + v0: tc * DK + v0 + vl],
                        st=(tc == 0), sp=(tc == ntc - 1):
                        E_.matmul(o, l, r, start=st, stop=sp),
                        reads=[b_atsb, b_vsb], writes=[b_obank[ob]])
            prog.op("vector",
                    lambda E_, o=x2pre[:, v0:v0 + vl], i=obank[ob][:, 0:vl],
                    xr=xres[slot][:, v0:v0 + vl], r=st2[par][:, 2:3]:
                    E_.scalar_tensor_tensor(o, i, r, xr, OP.mult, OP.add),
                    reads=[b_obank[ob], b_st2[par], b_xres[slot]],
                    writes=[b_x2pre])

        _emit_layernorm(prog, x2pre[:, 0:D], b_x2pre, sqtmp, b_sqtmp, lnst,
                        b_lnst, x2stage[slot][:, 0:D], b_x2stage[slot], D,
                        epsc, b_epsc)

        ch_x2s[slot].dma(x2_scr[q0:q0 + 128, :], x2stage[slot][:],
                         reads=[b_x2stage[slot]],
                         writes=[b_x2scr[qb // qb_per_tile]])
        ch_x2b[slot].dma(x2b_scr[q0:q0 + 128, :], x2stage[slot][:],
                         reads=[b_x2stage[slot]],
                         writes=[b_x2bscr[qb // qb_per_tile]])

    p2_load_qgroup(0)
    for qb in range(nqb):
        if qb % 4 == 0 and (qb // 4) + 1 <= (nqb - 1) // 4:
            p2_load_qgroup(qb // 4 + 1)
        p2_loads(qb)
        p2_s_stats(qb)
        if qb > 0:
            p2_attend(qb - 1)
    p2_attend(nqb - 1)

    p2ps.close()
    p2.close()
    pA.close()
    prog.barrier(all_chans)

    p3 = ExitStack()
    y_sb = sb("y_sb", [128, nj * D], F32, p3)
    x2fm = [sb(f"x2fm{i}", [128, ndc * TT], BF16, p3) for i in range(2)]
    h_sb = sb("h_sb", [128, nhc * TT], BF16, p3)
    w13_sb = [[sb(f"w1_{i}", [128, ndc * H], BF16, p3),
               sb(f"w3_{i}", [128, ndc * H], BF16, p3)] for i in range(2)]
    w2_sb = sb("w2_sb", [128, nhc * D], BF16, p3)
    t1r = [sb(f"t1r{i}", [128, colw], F32, p3) for i in range(3)]
    lg = sb("lg", [128, E], F32, p3)
    eg = sb("eg", [128, E], F32, p3)
    biased = sb("biased", [128, E], F32, p3)
    rankb = sb("rankb", [128, E], F32, p3)
    gtmp = sb("gtmp", [128, E], F32, p3)
    rw_all = sb("rw_all", [128, nj * E], F32, p3)
    rwp = sb("rwp", [128, EE], BF16, p3)
    rwt = sb("rwt", [EE, TT], BF16, p3)
    gstat = sb("gstat", [128, 8], F32, p3)
    lst = sb("lst", [128, 8], F32, p3)

    b_ysb, b_hsb = Buf("y_sb"), Buf("h_sb")
    b_x2fm = [Buf(f"x2fm{i}") for i in range(2)]
    b_w13 = [[Buf(f"w1_{i}"), Buf(f"w3_{i}")] for i in range(2)]
    b_w2 = Buf("w2_sb")
    b_t1r = [Buf(f"t1r{i}") for i in range(3)]
    b_lg, b_eg, b_biased = Buf("lg"), Buf("eg"), Buf("biased")
    b_rankb, b_gtmp = Buf("rankb"), Buf("gtmp")
    b_rw, b_rwp, b_rwt = Buf("rw_all"), Buf("rwp"), Buf("rwt")
    b_rwp_one = Buf("rwp_one")
    b_gstat = Buf("gstat")
    b_lst = Buf("lst")

    p3ps = ExitStack()
    hps = [ps(f"p3h{i}", [128, 512], F32, p3ps) for i in range(4)]
    b_hps = [Buf(f"p3h{i}") for i in range(4)]
    ops_ = [ps(f"p3o{i}", [128, 512], F32, p3ps) for i in range(2)]
    b_ops = [Buf(f"p3o{i}") for i in range(2)]
    rtp = ps("p3r", [128, 1024], BF16, p3ps)
    b_rtp = Buf("p3r")

    prog.op("vector", lambda E_: E_.memset(rwp[:, E:EE], 1.0), writes=[b_rwp_one])

    def wslot(e, tt):
        return (e + tt) % 2

    def load_w13(e, tt):
        s_ = wslot(e, tt)
        wsel, bsel = w13_sb[s_], b_w13[s_]
        ch = ch_w13[s_]
        ch.dma(wsel[0][:].rearrange("p (c h) -> p c h", c=ndc),
               w1_d[e].rearrange("(c p) h -> p c h", p=128), writes=[bsel[0]])
        ch.dma(wsel[1][:].rearrange("p (c h) -> p c h", c=ndc),
               w3_d[e].rearrange("(c p) h -> p c h", p=128), writes=[bsel[1]])
        tok = ch.batch_token()
        bsel[0].last_w = tok
        bsel[1].last_w = tok

    def load_w2e(e, tt):
        ch_w2[(e + tt) % 2].dma(w2_sb[:].rearrange("p (c d) -> p c d", c=nhc),
                                w2_d[e].rearrange("(c p) d -> p c d", p=128),
                                writes=[b_w2])

    def load_xf(tt):
        xs = tt % 2
        tb = tt * TT
        for dc in range(ndc):
            ch_xf[xs].dma(x2fm[xs][:, dc * TT:(dc + 1) * TT],
                          x2b_scr[tb:tb + TT, dc * 128:(dc + 1) * 128],
                          reads=[b_x2bscr[tt]], writes=[b_x2fm[xs]],
                          transpose=True)

    for tt in range(ntile):
        tb0 = tt * TT
        xslot = tt % 2
        if tt == 0:
            load_xf(0)
        ch_y.dma(y_sb[:].rearrange("p (j d) -> p j d", j=nj),
                 x2_scr[tb0:tb0 + TT, :].rearrange("(j p) d -> p j d", p=128),
                 reads=[b_x2scr[tt]], writes=[b_ysb])

        if tt == 0:
            load_w13(0, 0)
            load_w2e(0, 0)

        for j in range(nj):
            gb_ = ops_[j % 2]
            b_gb_ = b_ops[j % 2]
            for dc in range(ndc):
                prog.op("tensor",
                        lambda E_, o=gb_[:, 0:E],
                        l=x2fm[xslot][:, dc * TT + j * 128: dc * TT + (j + 1) * 128],
                        r=gwt[:, dc * E:(dc + 1) * E],
                        st=(dc == 0), sp=(dc == ndc - 1):
                        E_.matmul(o, l, r, start=st, stop=sp),
                        reads=[b_x2fm[xslot], b_gwt], writes=[b_gb_])
            prog.op("vector", lambda E_, i=gb_[:, 0:E], br=b_gb_:
                    E_.tensor_copy(lg[:], i),
                    reads=[b_gb_], writes=[b_lg])
            prog.op("vector",
                    lambda E_: E_.reduce_max(gstat[:, 0:1], lg[:], AX, negate=True),
                    reads=[b_lg], writes=[b_gstat])
            prog.op("scalar",
                    lambda E_: E_.activation(eg[:], lg[:], AF.Exp,
                                             bias=gstat[:, 0:1], scale=1.0),
                    reads=[b_lg, b_gstat], writes=[b_eg])
            prog.op("vector",
                    lambda E_: E_.reduce_sum(gstat[:, 1:2], eg[:], AX),
                    reads=[b_eg], writes=[b_gstat])
            prog.op("vector", lambda E_: E_.reciprocal(gstat[:, 2:3], gstat[:, 1:2]),
                    reads=[b_gstat], writes=[b_gstat])
            sc_ap = rw_all[:, j * E:(j + 1) * E]
            prog.op("vector",
                    lambda E_, o=sc_ap: E_.tensor_scalar(o, eg[:], gstat[:, 2:3],
                                                         None, OP.mult),
                    reads=[b_eg, b_gstat], writes=[b_rw])
            prog.op("vector",
                    lambda E_, o=sc_ap: E_.tensor_tensor(biased[:], o, gbrep[:],
                                                         OP.add),
                    reads=[b_rw, b_gbrep], writes=[b_biased])
            for e in range(E):
                prog.op("vector",
                        lambda E_, e=e: E_.tensor_scalar(gtmp[:], biased[:],
                                                         biased[:, e:e + 1],
                                                         None, OP.is_gt),
                        reads=[b_biased], writes=[b_gtmp])
                prog.op("vector",
                        lambda E_, e=e: E_.reduce_sum(rankb[:, e:e + 1], gtmp[:], AX),
                        reads=[b_gtmp], writes=[b_rankb])
            for e in range(1, E):
                prog.op("vector",
                        lambda E_, e=e: E_.tensor_scalar(gtmp[:, 0:e],
                                                         biased[:, 0:e],
                                                         biased[:, e:e + 1],
                                                         None, OP.is_equal),
                        reads=[b_biased], writes=[b_gtmp])
                prog.op("vector",
                        lambda E_, e=e: E_.reduce_sum(gstat[:, 3:4], gtmp[:, 0:e], AX),
                        reads=[b_gtmp], writes=[b_gstat])
                prog.op("vector",
                        lambda E_, e=e: E_.tensor_tensor(rankb[:, e:e + 1],
                                                         rankb[:, e:e + 1],
                                                         gstat[:, 3:4], OP.add),
                        reads=[b_gstat, b_rankb], writes=[b_rankb])
            prog.op("vector",
                    lambda E_: E_.tensor_scalar(gtmp[:], rankb[:], float(TOPK),
                                                None, OP.is_lt),
                    reads=[b_rankb], writes=[b_gtmp])
            prog.op("vector",
                    lambda E_, o=sc_ap: E_.tensor_tensor(o, o, gtmp[:], OP.mult),
                    reads=[b_gtmp, b_rw], writes=[b_rw])

        if tt + 1 < ntile:
            load_xf(tt + 1)

        for e in range(EE):
            wsel, bsel = w13_sb[wslot(e, tt)], b_w13[wslot(e, tt)]
            if e < E:
                load_w13(e + 1, tt)
            elif tt + 1 < ntile:
                load_w13(0, tt + 1)

            for hc in range(nhc):
                for cc in range(ncol):
                    c0 = cc * colw
                    pb = (hc * ncol + cc) % 2
                    p1b, p3b = hps[2 * pb], hps[2 * pb + 1]
                    bp1, bp3 = b_hps[2 * pb], b_hps[2 * pb + 1]
                    for dc in range(ndc):
                        prog.op("tensor",
                                lambda E_, o=p1b[:, 0:colw],
                                l=wsel[0][:, dc * H + hc * 128: dc * H + (hc + 1) * 128],
                                r=x2fm[xslot][:, dc * TT + c0: dc * TT + c0 + colw],
                                st=(dc == 0), sp=(dc == ndc - 1):
                                E_.matmul(o, l, r, start=st, stop=sp),
                                reads=[bsel[0], b_x2fm[xslot]], writes=[bp1])
                    for dc in range(ndc):
                        prog.op("tensor",
                                lambda E_, o=p3b[:, 0:colw],
                                l=wsel[1][:, dc * H + hc * 128: dc * H + (hc + 1) * 128],
                                r=x2fm[xslot][:, dc * TT + c0: dc * TT + c0 + colw],
                                st=(dc == 0), sp=(dc == ndc - 1):
                                E_.matmul(o, l, r, start=st, stop=sp),
                                reads=[bsel[1], b_x2fm[xslot]], writes=[bp3])
                    tsl = (hc * ncol + cc) % 3
                    prog.op("vector",
                            lambda E_, o=t1r[tsl][:], i=p1b[:, 0:colw],
                            bb=b1c[:, e * nhc + hc: e * nhc + hc + 1]:
                            E_.tensor_scalar(o, i, bb, None, OP.add),
                            reads=[bp1, b_b1c], writes=[b_t1r[tsl]])
                    prog.op("vector",
                            lambda E_, o=t1r[tsl][:], i=p3b[:, 0:colw],
                            t1=t1r[tsl][:],
                            bb=b3c[:, e * nhc + hc: e * nhc + hc + 1]:
                            E_.scalar_tensor_tensor(o, i, bb, t1, OP.add, OP.mult),
                            reads=[bp3, b_b3c, b_t1r[tsl]], writes=[b_t1r[tsl]])
                    prog.op("scalar",
                            lambda E_, o=h_sb[:, hc * TT + c0: hc * TT + c0 + colw],
                            i=t1r[tsl][:]: E_.activation(o, i, AF.Silu),
                            reads=[b_t1r[tsl]], writes=[b_hsb])

            if e == 0:
                for j in range(nj):
                    prog.op("vector",
                            lambda E_, o=rwp[:, 0:E], i=rw_all[:, j * E:(j + 1) * E]:
                            E_.tensor_copy(o, i),
                            reads=[b_rw], writes=[b_rwp])
                    prog.op("tensor",
                            lambda E_, o=rtp[0:EE, j * 128:(j + 1) * 128], i=rwp[:, 0:EE]:
                            E_.transpose(o, i, idb[:]),
                            reads=[b_rwp, b_rwp_one, b_idb], writes=[b_rtp])
                prog.op("vector",
                        lambda E_: E_.tensor_copy(rwt[0:EE, 0:TT], rtp[0:EE, 0:TT]),
                        reads=[b_rtp], writes=[b_rwt])
                for j in range(nj):
                    for (v0, vl) in splits(D, 512):
                        ob = (j * ((D + 511) // 512) + v0 // 512) % 2
                        prog.op("tensor",
                                lambda E_, o=ops_[ob][:, 0:vl],
                                l=rwt[0:EE, j * 128:(j + 1) * 128],
                                r=b2all[0:EE, v0:v0 + vl]:
                                E_.matmul(o, l, r, start=True, stop=True),
                                reads=[b_rwt, b_b2all], writes=[b_ops[ob]])
                        prog.op("vector",
                                lambda E_, o=y_sb[:, j * D + v0: j * D + v0 + vl],
                                i=ops_[ob][:, 0:vl]:
                                E_.tensor_tensor(o, o, i, OP.add),
                                reads=[b_ops[ob], b_ysb], writes=[b_ysb])

            for j in range(nj):
                for (v0, vl) in splits(D, 512):
                    ob = (j * ((D + 511) // 512) + v0 // 512) % 2
                    for hc in range(nhc):
                        prog.op("tensor",
                                lambda E_, o=ops_[ob][:, 0:vl],
                                l=h_sb[:, hc * TT + j * 128: hc * TT + (j + 1) * 128],
                                r=w2_sb[:, hc * D + v0: hc * D + v0 + vl],
                                st=(hc == 0), sp=(hc == nhc - 1):
                                E_.matmul(o, l, r, start=st, stop=sp),
                                reads=[b_hsb, b_w2], writes=[b_ops[ob]])
                    if e < E:
                        prog.op("vector",
                                lambda E_, o=y_sb[:, j * D + v0: j * D + v0 + vl],
                                i=ops_[ob][:, 0:vl],
                                rc=rw_all[:, j * E + e: j * E + e + 1]:
                                E_.scalar_tensor_tensor(o, i, rc, o, OP.mult, OP.add),
                                reads=[b_ops[ob], b_rw, b_ysb], writes=[b_ysb])
                    else:
                        prog.op("vector",
                                lambda E_, o=y_sb[:, j * D + v0: j * D + v0 + vl],
                                i=ops_[ob][:, 0:vl]:
                                E_.tensor_tensor(o, o, i, OP.add),
                                reads=[b_ops[ob], b_ysb], writes=[b_ysb])
                if e == EE - 1:
                    _emit_layernorm(prog, y_sb[:, j * D:(j + 1) * D], b_ysb,
                                    t1r[0], b_t1r[0], lst, b_lst,
                                    y_sb[:, j * D:(j + 1) * D], b_ysb, D,
                                    epsc, b_epsc)
                    ch_out[j % 2].dma(
                        out_d[tb0 + j * 128: tb0 + (j + 1) * 128, :],
                        y_sb[:, j * D:(j + 1) * D], reads=[b_ysb])

            if e < EE - 1:
                load_w2e(e + 1, tt)
            elif tt + 1 < ntile:
                load_w2e(0, tt + 1)

    p3ps.close()

    end = []
    for ch in ch_out:
        if ch.cnt:
            end.append(Tok(ch.sem, ch.key, 16 * ch.cnt))
    prog.finalize(end)
    p3.close()
    return nc, es


def kernel(L0_attn_X, L0_attn_W_q, L0_attn_W_k, L0_attn_W_v,
           L0_attn_b_q, L0_attn_b_k, L0_attn_b_v,
           L0_ffn_gate_w, L0_ffn_gate_b,
           L0_ffn_w1, L0_ffn_b1, L0_ffn_w3, L0_ffn_b3, L0_ffn_w2, L0_ffn_b2,
           L0_ffn_sw1, L0_ffn_sb1, L0_ffn_sw3, L0_ffn_sb3, L0_ffn_sw2, L0_ffn_sb2):
    B, S, D = L0_attn_X.shape
    cfg = dict(S=S, D=D, DK=D, H=int(np.asarray(L0_ffn_w1).shape[2]),
               E=int(np.asarray(L0_ffn_w1).shape[0]),
               TT=1024, scale=1.0 / float(np.sqrt(D)), TOPK=2)

    asnp = lambda a: np.ascontiguousarray(np.asarray(a))
    X = asnp(L0_attn_X)
    bqkv = np.stack([asnp(L0_attn_b_q), asnp(L0_attn_b_k), asnp(L0_attn_b_v)], axis=0)
    w1s = np.concatenate([asnp(L0_ffn_w1), asnp(L0_ffn_sw1)[None]], axis=0)
    w3s = np.concatenate([asnp(L0_ffn_w3), asnp(L0_ffn_sw3)[None]], axis=0)
    w2s = np.concatenate([asnp(L0_ffn_w2), asnp(L0_ffn_sw2)[None]], axis=0)
    b1s = np.concatenate([asnp(L0_ffn_b1), asnp(L0_ffn_sb1)[None]], axis=0)
    b3s = np.concatenate([asnp(L0_ffn_b3), asnp(L0_ffn_sb3)[None]], axis=0)
    b2s = np.concatenate([asnp(L0_ffn_b2), asnp(L0_ffn_sb2)[None]], axis=0)

    shared = {
        "wq": asnp(L0_attn_W_q), "wk": asnp(L0_attn_W_k), "wv": asnp(L0_attn_W_v),
        "bqkv": bqkv, "gw": asnp(L0_ffn_gate_w), "gb": asnp(L0_ffn_gate_b)[None, :],
        "w1s": w1s, "w3s": w3s, "w2s": w2s, "b1s": b1s, "b3s": b3s, "b2s": b2s,
        "ident_bf": np.eye(128, dtype=BF), "ident_f": np.eye(128, dtype=np.float32),
        "ones1": np.ones((1, 128), dtype=np.float32),
    }

    nc, es = build_core_program(cfg)
    in_maps = [dict(shared, x=np.ascontiguousarray(X[c])) for c in range(B)]
    res = run_bass_kernel_spmd(nc, in_maps, core_ids=list(range(B)))
    out = np.stack([np.asarray(res.results[c]["out"]) for c in range(B)], axis=0)
    es.close()
    return out


# version 16
# speedup vs baseline: 1.1274x; 1.1274x over previous
import numpy as np
import ml_dtypes
from contextlib import ExitStack

import concourse.bass as bass
import concourse.mybir as mybir
from concourse.bass_utils import run_bass_kernel_spmd

F32 = mybir.dt.float32
BF16 = mybir.dt.bfloat16
AX = mybir.AxisListType.X
OP = mybir.AluOpType
AF = mybir.ActivationFunctionType
BF = ml_dtypes.bfloat16


class Tok:
    __slots__ = ("sem", "key", "val", "needed", "ival")

    def __init__(self, sem, key, val, needed=True):
        self.sem = sem
        self.key = key
        self.val = val
        self.needed = needed
        self.ival = val


class Buf:
    __slots__ = ("name", "last_w", "readers")

    def __init__(self, name):
        self.name = name
        self.last_w = None
        self.readers = {}


class Chan:

    def __init__(self, prog, queue, name):
        self.prog = prog
        self.queue = queue
        self.sem = prog.es.enter_context(prog.nc.semaphore(name))
        self.key = ("c", name)
        self.cnt = 0

    def dma(self, out_ap, in_ap, reads=(), writes=(), transpose=False, **kw):
        p = self.prog
        deps = p._deps(reads, writes)
        p._resolve(self.queue, deps)
        self.cnt += 1
        v = 16 * self.cnt
        sem = self.sem
        if transpose:
            p.cmds[self.queue].append(
                lambda eng, o=out_ap, i=in_ap, s=sem:
                eng.dma_start_transpose(o, i).then_inc(s, 16))
        else:
            p.cmds[self.queue].append(
                lambda eng, o=out_ap, i=in_ap, s=sem, kw=dict(kw):
                eng.dma_start(o, i, **kw).then_inc(s, 16))
        t = Tok(sem, self.key, v)
        p._note(t, reads, writes)
        return t

    def batch_token(self):
        return Tok(self.sem, self.key, 16 * self.cnt)


class Prog:
    ENGINES = ("tensor", "vector", "scalar", "sync", "gpsimd")

    def __init__(self, nc, es):
        self.nc = nc
        self.es = es
        self.cmds = {e: [] for e in self.ENGINES}
        self.sem = {}
        self.cnt = {}
        self.toks = {}
        self.last_tok = {}
        for e in ("tensor", "vector", "scalar"):
            self.sem[e] = es.enter_context(nc.semaphore("sem_" + e))
            self.cnt[e] = 0
            self.toks[e] = []
            self.last_tok[e] = None
        self.wm = {e: {} for e in self.ENGINES}

    def _deps(self, reads, writes):
        deps = []
        for b in reads:
            if b.last_w is not None:
                deps.append(b.last_w)
        for b in writes:
            if b.last_w is not None:
                deps.append(b.last_w)
            deps.extend(b.readers.values())
        return deps

    def _resolve(self, eng, deps):
        wm = self.wm[eng]
        own = ("e", eng)
        for t in deps:
            if t.key == own and eng == "tensor":
                continue
            if wm.get(t.key, 0) >= t.val:
                continue
            wm[t.key] = t.val
            t.needed = True
            self.cmds[eng].append(
                lambda E, t=t: E.wait_ge(t.sem, t.ival))

    def _note(self, tok, reads, writes):
        for b in reads:
            r = b.readers.get(tok.key)
            if r is None or r.val < tok.val:
                b.readers[tok.key] = tok
        for b in writes:
            b.last_w = tok
            b.readers = {}

    def op(self, eng, fn, reads=(), writes=(), deps=()):
        dl = self._deps(reads, writes)
        dl.extend(deps)
        self._resolve(eng, dl)
        self.cnt[eng] += 1
        sem = self.sem[eng]
        v = self.cnt[eng]
        t = Tok(sem, ("e", eng), v, needed=False)
        self.toks[eng].append(t)
        self.last_tok[eng] = t
        self.cmds[eng].append(
            lambda E, f=fn, t=t: f(E).then_inc(t.sem, 1) if t.needed else f(E))
        self._note(t, reads, writes)
        return t

    def barrier(self, chans):
        toks = []
        for e in ("tensor", "vector", "scalar"):
            if self.last_tok[e] is not None:
                toks.append(self.last_tok[e])
        for ch in chans:
            if ch.cnt:
                toks.append(ch.batch_token())
        for e in self.ENGINES:
            self._resolve(e, toks)

    def finalize(self, end_toks):
        for t in end_toks:
            self.cmds["sync"].append(
                lambda E, s=t.sem, v=t.val: E.wait_ge(s, v))
        for e, toks in self.toks.items():
            n = 0
            for t in toks:
                if t.needed:
                    n += 1
                t.ival = n
        with self.nc.Block() as block:
            @block.tensor
            def _(eng):
                for f in self.cmds["tensor"]:
                    f(eng)

            @block.vector
            def _(eng):
                for f in self.cmds["vector"]:
                    f(eng)

            @block.scalar
            def _(eng):
                for f in self.cmds["scalar"]:
                    f(eng)

            @block.sync
            def _(eng):
                for f in self.cmds["sync"]:
                    f(eng)

            @block.gpsimd
            def _(eng):
                for f in self.cmds["gpsimd"]:
                    f(eng)


def splits(total, size):
    return [(s, min(size, total - s)) for s in range(0, total, size)]


def _emit_layernorm(prog, x_ap, b_x, sqb, b_sqb, stt, b_stt, out_ap, b_out, D,
                    epsc, b_epsc):
    prog.op("vector", lambda E_: E_.reduce_sum(stt[:, 0:1], x_ap, AX),
            reads=[b_x], writes=[b_stt])
    ncut = (D + 511) // 512
    for ci in range(ncut):
        c0 = ci * 512
        cl = min(512, D - c0)
        acc_col = stt[:, 1:2] if ci == 0 else stt[:, 2:3]
        prog.op("scalar",
                lambda E_, o=sqb[:, 0:cl], i=x_ap[:, c0:c0 + cl]:
                E_.activation(o, i, AF.Square),
                reads=[b_x], writes=[b_sqb])
        prog.op("vector",
                lambda E_, o=acc_col, i=sqb[:, 0:cl]: E_.reduce_sum(o, i, AX),
                reads=[b_sqb], writes=[b_stt])
    if ncut == 2:
        prog.op("vector", lambda E_: E_.tensor_tensor(stt[:, 2:3], stt[:, 1:2],
                                                      stt[:, 2:3], OP.add),
                reads=[b_stt], writes=[b_stt])
    else:
        prog.op("vector", lambda E_: E_.tensor_copy(stt[:, 2:3], stt[:, 1:2]),
                reads=[b_stt], writes=[b_stt])
    inv = 1.0 / D
    prog.op("vector", lambda E_: E_.tensor_scalar(stt[:, 3:4], stt[:, 0:1], inv,
                                                  None, OP.mult),
            reads=[b_stt], writes=[b_stt])
    prog.op("vector", lambda E_: E_.tensor_tensor(stt[:, 4:5], stt[:, 3:4],
                                                  stt[:, 3:4], OP.mult),
            reads=[b_stt], writes=[b_stt])
    prog.op("vector", lambda E_: E_.tensor_scalar(stt[:, 5:6], stt[:, 2:3], inv,
                                                  stt[:, 4:5], OP.mult, OP.subtract),
            reads=[b_stt], writes=[b_stt])
    prog.op("scalar", lambda E_: E_.activation(stt[:, 5:6], stt[:, 5:6], AF.Ln,
                                               bias=epsc[:, 0:1], scale=1.0),
            reads=[b_stt, b_epsc], writes=[b_stt])
    prog.op("scalar", lambda E_: E_.activation(stt[:, 6:7], stt[:, 5:6], AF.Exp,
                                               bias=0.0, scale=-0.5),
            reads=[b_stt], writes=[b_stt])
    prog.op("vector", lambda E_: E_.scalar_tensor_tensor(stt[:, 7:8], stt[:, 3:4],
                                                         -1.0, stt[:, 6:7],
                                                         OP.mult, OP.mult),
            reads=[b_stt], writes=[b_stt])
    prog.op("vector", lambda E_: E_.tensor_scalar(out_ap, x_ap, stt[:, 6:7],
                                                  stt[:, 7:8], OP.mult, OP.add),
            reads=[b_x, b_stt], writes=[b_out])


def build_core_program(cfg):
    S = cfg["S"]
    D = cfg["D"]
    DK = cfg["DK"]
    H = cfg["H"]
    E = cfg["E"]
    TT = cfg["TT"]
    scale = cfg["scale"]
    TOPK = cfg.get("TOPK", 2)
    EPS = 1e-5

    ndc = D // 128
    nkc = DK // 128
    nhc = H // 128
    ntb = S // 512
    nqb = S // 128
    ntile = S // TT
    nj = TT // 128
    ncol = TT // 512 if TT >= 512 else 1
    colw = min(512, TT)
    EE = E + 1
    nsc = S // 512
    ntc = S // 128
    natg = (ntc + 3) // 4
    qb_per_tile = nqb // ntile

    nc = bass.Bass("TRN2", target_bir_lowering=False)

    x_d = nc.dram_tensor("x", [S, D], F32, kind="ExternalInput").ap()
    wq_d = nc.dram_tensor("wq", [D, DK], F32, kind="ExternalInput").ap()
    wk_d = nc.dram_tensor("wk", [D, DK], F32, kind="ExternalInput").ap()
    wv_d = nc.dram_tensor("wv", [D, DK], F32, kind="ExternalInput").ap()
    bqkv_d = nc.dram_tensor("bqkv", [3, DK], F32, kind="ExternalInput").ap()
    gw_d = nc.dram_tensor("gw", [E, D], F32, kind="ExternalInput").ap()
    gb_d = nc.dram_tensor("gb", [1, E], F32, kind="ExternalInput").ap()
    w1_d = nc.dram_tensor("w1s", [EE, D, H], F32, kind="ExternalInput").ap()
    w3_d = nc.dram_tensor("w3s", [EE, D, H], F32, kind="ExternalInput").ap()
    w2_d = nc.dram_tensor("w2s", [EE, H, D], F32, kind="ExternalInput").ap()
    b1_d = nc.dram_tensor("b1s", [EE, H], F32, kind="ExternalInput").ap()
    b3_d = nc.dram_tensor("b3s", [EE, H], F32, kind="ExternalInput").ap()
    b2_d = nc.dram_tensor("b2s", [EE, D], F32, kind="ExternalInput").ap()
    idb_d = nc.dram_tensor("ident_bf", [128, 128], BF16, kind="ExternalInput").ap()
    idf_d = nc.dram_tensor("ident_f", [128, 128], F32, kind="ExternalInput").ap()
    ones_d = nc.dram_tensor("ones1", [1, 128], F32, kind="ExternalInput").ap()

    out_d = nc.dram_tensor("out", [S, D], F32, kind="ExternalOutput").ap()

    q_scr = nc.dram_tensor("q_scr", [ntb, 128, nkc * 512], BF16, kind="Internal").ap()
    x2_scr = nc.dram_tensor("x2_scr", [S, D], F32, kind="Internal").ap()
    x2b_scr = nc.dram_tensor("x2b_scr", [S, D], BF16, kind="Internal").ap()

    es = ExitStack()
    prog = Prog(nc, es)

    def sb(name, shape, dt, scope=es):
        return scope.enter_context(nc.sbuf_tensor(name, shape, dt))

    def ps(name, shape, dt, scope=es):
        return scope.enter_context(nc.psum_tensor(name, shape, dt))

    ch_const = Chan(prog, "sync", "ch_const")
    ch_qs = [Chan(prog, "sync", f"ch_qs{i}") for i in range(2)]
    ch_q = [Chan(prog, "sync", f"ch_q{i}") for i in range(2)]
    ch_xr = [Chan(prog, "sync", f"ch_xr{i}") for i in range(2)]
    ch_x2s = [Chan(prog, "sync", f"ch_x2s{i}") for i in range(2)]
    ch_y = Chan(prog, "sync", "ch_y")
    ch_xf = [Chan(prog, "sync", f"ch_xf{i}") for i in range(2)]
    ch_out = [Chan(prog, "sync", f"ch_out{i}") for i in range(2)]
    ch_xin = Chan(prog, "gpsimd", "ch_xin")
    ch_x2b = [Chan(prog, "gpsimd", f"ch_x2b{i}") for i in range(2)]
    ch_w13 = [Chan(prog, "gpsimd", f"ch_w13{i}") for i in range(2)]
    ch_w2 = [Chan(prog, "gpsimd", f"ch_w2{i}") for i in range(2)]
    ch_wqkv = Chan(prog, "gpsimd", "ch_wqkv")
    all_chans = ([ch_const, ch_y, ch_xin, ch_wqkv]
                 + ch_qs + ch_q + ch_xr + ch_x2s + ch_out + ch_x2b
                 + ch_w13 + ch_w2 + ch_xf)

    b_qscr = Buf("q_scr")
    b_x2scr = [Buf(f"x2_scr{t}") for t in range(ntile)]
    b_x2bscr = [Buf(f"x2b_scr{t}") for t in range(ntile)]

    idb = sb("idb", [128, 128], BF16)
    gwt = sb("gwt", [128, ndc * E], BF16)
    gbrep = sb("gbrep", [128, E], F32)
    b1c = sb("b1c", [128, EE * nhc], F32)
    b3c = sb("b3c", [128, EE * nhc], F32)
    b2all = sb("b2all", [EE, D], BF16)
    epsc = sb("epsc", [128, 1], F32)
    b_idb, b_idf, b_ones = Buf("idb"), Buf("idf"), Buf("ones1")
    b_gwt, b_gbrep = Buf("gwt"), Buf("gbrep")
    b_b1c, b_b3c, b_b2all = Buf("b1c"), Buf("b3c"), Buf("b2all")
    b_epsc = Buf("epsc")

    pA = ExitStack()
    k_sb = sb("k_sb", [128, nkc * S], BF16, pA)
    v_sb = sb("v_sb", [128, ntc * DK], BF16, pA)
    b_ksb, b_vsb = Buf("k_sb"), Buf("v_sb")

    pro_small = ExitStack()
    idf = sb("idf", [128, 128], F32, pro_small)
    ones1 = sb("ones_s", [1, 128], F32, pro_small)
    bqc = sb("bqc", [128, nkc], F32, pro_small)
    bkc = sb("bkc", [128, nkc], F32, pro_small)
    bvrep = sb("bvrep", [128, DK], BF16, pro_small)
    b_bqc, b_bkc, b_bvrep = Buf("bqc"), Buf("bkc"), Buf("bvrep")

    pro = ExitStack()
    gw_sb = sb("gw_sb", [E, D], F32, pro)
    b1stage = sb("b1stage", [EE, H], F32, pro)
    b3stage = sb("b3stage", [EE, H], F32, pro)
    bq_sb = sb("bq_sb", [1, DK], F32, pro)
    bk_sb = sb("bk_sb", [1, DK], F32, pro)
    bv_sb = sb("bv_sb", [1, DK], F32, pro)
    gb_sb = sb("gb_sb", [1, E], F32, pro)
    b_gw, b_b1s, b_b3s = Buf("gw_sb"), Buf("b1stage"), Buf("b3stage")
    b_bqkv, b_gb = Buf("bqkv"), Buf("gb_sb")

    prog.op("vector", lambda E_: E_.memset(epsc[:], float(EPS)), writes=[b_epsc])
    ch_const.dma(idb[:], idb_d[:], writes=[b_idb])
    ch_const.dma(idf[:], idf_d[:], writes=[b_idf])
    ch_const.dma(ones1[:], ones_d[:], writes=[b_ones])
    ch_const.dma(gw_sb[:], gw_d[:], writes=[b_gw])
    ch_const.dma(b1stage[:], b1_d[:], writes=[b_b1s])
    ch_const.dma(b3stage[:], b3_d[:], writes=[b_b3s])
    ch_const.dma(bq_sb[:], bqkv_d[0:1, :], writes=[b_bqkv])
    ch_const.dma(bk_sb[:], bqkv_d[1:2, :], writes=[b_bqkv])
    ch_const.dma(bv_sb[:], bqkv_d[2:3, :], writes=[b_bqkv])
    ch_const.dma(gb_sb[:], gb_d[:], writes=[b_gb])
    tok_const = ch_const.batch_token()
    for b in (b_idb, b_idf, b_ones, b_gw, b_b1s, b_b3s, b_bqkv, b_gb):
        b.last_w = tok_const

    ch_wqkv.dma(b2all[:], b2_d[:], writes=[b_b2all])

    pps = ExitStack()
    pf = [ps(f"pro_f{i}", [128, 512], F32, pps) for i in range(2)]
    b_pf = [Buf(f"pro_f{i}") for i in range(2)]

    for dc in range(ndc):
        bank = dc % 2
        prog.op("tensor",
                lambda E_, o=pf[bank][0:128, 0:E],
                i=gw_sb[0:E, dc * 128:(dc + 1) * 128], ii=idf[0:E, 0:E]:
                E_.transpose(o, i, ii),
                reads=[b_gw, b_idf], writes=[b_pf[bank]])
        prog.op("vector",
                lambda E_, o=gwt[:, dc * E:(dc + 1) * E], i=pf[bank][0:128, 0:E]:
                E_.tensor_copy(o, i),
                reads=[b_pf[bank]], writes=[b_gwt])
    prog.op("tensor",
            lambda E_: E_.matmul(pf[0][0:128, 0:E], ones1[0:1, :], gb_sb[0:1, :],
                                 start=True, stop=True),
            reads=[b_ones, b_gb], writes=[b_pf[0]])
    prog.op("vector", lambda E_: E_.tensor_copy(gbrep[:], pf[0][0:128, 0:E]),
            reads=[b_pf[0]], writes=[b_gbrep])

    for (stage, bstage, dest, bdest) in ((b1stage, b_b1s, b1c, b_b1c),
                                         (b3stage, b_b3s, b3c, b_b3c)):
        for hc in range(nhc):
            bank = hc % 2
            prog.op("tensor",
                    lambda E_, o=pf[bank][0:128, 0:EE],
                    i=stage[0:EE, hc * 128:(hc + 1) * 128], ii=idf[0:EE, 0:EE]:
                    E_.transpose(o, i, ii),
                    reads=[bstage, b_idf], writes=[b_pf[bank]])
            prog.op("vector",
                    lambda E_,
                    o=dest[:].rearrange("p (e c) -> p e c", e=EE)[:, :, hc],
                    i=pf[bank][0:128, 0:EE]: E_.tensor_copy(o, i),
                    reads=[b_pf[bank]], writes=[bdest])

    for (bsrc, dest, bdest) in ((bq_sb, bqc, b_bqc), (bk_sb, bkc, b_bkc)):
        for kc in range(nkc):
            bank = kc % 2
            prog.op("tensor",
                    lambda E_, o=pf[bank][0:128, 0:1],
                    i=bsrc[0:1, kc * 128:(kc + 1) * 128], ii=idf[0:1, 0:1]:
                    E_.transpose(o, i, ii),
                    reads=[b_bqkv, b_idf], writes=[b_pf[bank]])
            prog.op("vector",
                    lambda E_, o=dest[:, kc:kc + 1], i=pf[bank][0:128, 0:1]:
                    E_.tensor_copy(o, i),
                    reads=[b_pf[bank]], writes=[bdest])
    prog.op("vector",
            lambda E_: E_.tensor_scalar(bqc[:], bqc[:], float(scale), None, OP.mult),
            reads=[b_bqc], writes=[b_bqc])

    for (c0, cl) in splits(DK, 512):
        bank = (c0 // 512) % 2
        prog.op("tensor",
                lambda E_, o=pf[bank][0:128, 0:cl], i=bv_sb[0:1, c0:c0 + cl]:
                E_.matmul(o, ones1[0:1, :], i, start=True, stop=True),
                reads=[b_ones, b_bqkv], writes=[b_pf[bank]])
        prog.op("vector",
                lambda E_, o=bvrep[:, c0:c0 + cl], i=pf[bank][0:128, 0:cl]:
                E_.tensor_copy(o, i),
                reads=[b_pf[bank]], writes=[b_bvrep])

    pps.close()
    prog.barrier(all_chans)
    pro.close()

    p1 = ExitStack()
    wk_sb = sb("wk_sb", [128, ndc * DK], BF16, p1)
    wv_sb = sb("wv_sb", [128, ndc * DK], BF16, p1)
    wq_sb = sb("wq_sb", [128, ndc * DK], BF16, p1)
    xblk = sb("xblk", [128, 4 * D], BF16, p1)
    xfm = sb("xfm", [128, ndc * 512], BF16, p1)
    qstage = [sb(f"qstage{i}", [128, 512], BF16, p1) for i in range(2)]
    b_wk, b_wv, b_wq = Buf("wk"), Buf("wv"), Buf("wq")
    b_xblk, b_xfm = Buf("xblk"), Buf("xfm")
    b_qstage = [Buf(f"qstage{i}") for i in range(2)]

    for (w_d_, w_sb_, b_w_) in ((wk_d, wk_sb, b_wk), (wv_d, wv_sb, b_wv),
                                (wq_d, wq_sb, b_wq)):
        ch_wqkv.dma(w_sb_[:].rearrange("p (c k) -> p c k", c=ndc),
                    w_d_.rearrange("(c p) k -> p c k", p=128),
                    writes=[b_w_])
    tok_wqkv = ch_wqkv.batch_token()
    for b in (b_b2all, b_wk, b_wv, b_wq):
        b.last_w = tok_wqkv

    p1ps = ExitStack()
    f32b = [ps(f"p1f{i}", [128, 512], F32, p1ps) for i in range(3)]
    b_f32b = [Buf(f"p1f{i}") for i in range(3)]
    bf16b = [ps(f"p1b{i}", [128, 1024], BF16, p1ps) for i in range(2)]
    b_bf16b = [Buf(f"p1b{i}") for i in range(2)]

    for blk in range(ntb):
        t0 = blk * 512
        ch_xin.dma(xblk[:].rearrange("p (c d) -> p c d", c=4),
                   x_d[t0:t0 + 512, :].rearrange("(c p) d -> p c d", p=128),
                   writes=[b_xblk])
        for dc in range(ndc):
            bank = dc % 2
            for tc in range(4):
                prog.op("tensor",
                        lambda E_, o=bf16b[bank][:, tc * 128:(tc + 1) * 128],
                        i=xblk[:, tc * D + dc * 128: tc * D + (dc + 1) * 128]:
                        E_.transpose(o, i, idb[:]),
                        reads=[b_xblk, b_idb], writes=[b_bf16b[bank]])
            prog.op("vector",
                    lambda E_, o=xfm[:, dc * 512:(dc + 1) * 512],
                    i=bf16b[bank][:, 0:512]: E_.tensor_copy(o, i),
                    reads=[b_bf16b[bank]], writes=[b_xfm])

        for kc in range(nkc):
            bank = kc % 3
            for dc in range(ndc):
                prog.op("tensor",
                        lambda E_, o=f32b[bank][:],
                        l=wk_sb[:, dc * DK + kc * 128: dc * DK + (kc + 1) * 128],
                        r=xfm[:, dc * 512:(dc + 1) * 512],
                        st=(dc == 0), sp=(dc == ndc - 1):
                        E_.matmul(o, l, r, start=st, stop=sp),
                        reads=[b_wk, b_xfm], writes=[b_f32b[bank]])
            prog.op("vector",
                    lambda E_, o=k_sb[:, kc * S + t0: kc * S + t0 + 512],
                    i=f32b[bank][:], bb=bkc[:, kc:kc + 1]:
                    E_.tensor_scalar(o, i, bb, None, OP.add),
                    reads=[b_f32b[bank], b_bkc], writes=[b_ksb])

        for tc in range(4):
            for (v0, vl) in splits(DK, 512):
                bank = (tc * 2 + v0 // 512) % 3
                for dc in range(ndc):
                    prog.op("tensor",
                            lambda E_, o=f32b[bank][:, 0:vl],
                            l=xfm[:, dc * 512 + tc * 128: dc * 512 + (tc + 1) * 128],
                            r=wv_sb[:, dc * DK + v0: dc * DK + v0 + vl],
                            st=(dc == 0), sp=(dc == ndc - 1):
                            E_.matmul(o, l, r, start=st, stop=sp),
                            reads=[b_wv, b_xfm], writes=[b_f32b[bank]])
                tch = blk * 4 + tc
                prog.op("vector",
                        lambda E_, o=v_sb[:, tch * DK + v0: tch * DK + v0 + vl],
                        i=f32b[bank][:, 0:vl], bb=bvrep[:, v0:v0 + vl]:
                        E_.tensor_tensor(o, i, bb, OP.add),
                        reads=[b_f32b[bank], b_bvrep], writes=[b_vsb])

        for kc in range(nkc):
            bank = kc % 3
            slot = kc % 2
            for dc in range(ndc):
                prog.op("tensor",
                        lambda E_, o=f32b[bank][:],
                        l=wq_sb[:, dc * DK + kc * 128: dc * DK + (kc + 1) * 128],
                        r=xfm[:, dc * 512:(dc + 1) * 512],
                        st=(dc == 0), sp=(dc == ndc - 1):
                        E_.matmul(o, l, r, start=st, stop=sp),
                        reads=[b_wq, b_xfm], writes=[b_f32b[bank]])
            prog.op("vector",
                    lambda E_, o=qstage[slot][:], i=f32b[bank][:],
                    bb=bqc[:, kc:kc + 1]:
                    E_.tensor_scalar(o, i, float(scale), bb, OP.mult, OP.add),
                    reads=[b_f32b[bank], b_bqc], writes=[b_qstage[slot]])
            ch_qs[slot].dma(q_scr[blk, :, kc * 512:(kc + 1) * 512], qstage[slot][:],
                            reads=[b_qstage[slot]], writes=[b_qscr])

    p1ps.close()
    p1.close()
    pro_small.close()
    prog.barrier(all_chans)

    p2 = ExitStack()
    s_sb = [sb(f"s_sb{i}", [128, S], BF16, p2) for i in range(2)]
    a_sb = sb("a_sb", [128, S], BF16, p2)
    at_sb = sb("at_sb", [128, S], BF16, p2)
    qblk = [sb(f"qblk{i}", [128, nkc * 512], BF16, p2) for i in range(2)]
    xres = [sb(f"xres{i}", [128, D], F32, p2) for i in range(2)]
    x2pre = sb("x2pre", [128, D], F32, p2)
    sqtmp = sb("sqtmp", [128, min(512, D)], F32, p2)
    x2stage = [sb(f"x2stage{i}", [128, D], F32, p2) for i in range(2)]
    stats = [sb(f"stats{i}", [128, 16], F32, p2) for i in range(2)]
    st2 = [sb(f"st2_{i}", [128, 8], F32, p2) for i in range(2)]
    lnst = sb("lnst", [128, 8], F32, p2)
    b_ssb = [Buf(f"s_sb{i}") for i in range(2)]
    b_asb, b_atsb = Buf("a_sb"), Buf("at_sb")
    b_qblk = [Buf(f"qblk{i}") for i in range(2)]
    b_xres = [Buf(f"xres{i}") for i in range(2)]
    b_x2pre, b_sqtmp = Buf("x2pre"), Buf("sqtmp")
    b_x2stage = [Buf(f"x2stage{i}") for i in range(2)]
    b_stats = [Buf(f"stats{i}") for i in range(2)]
    b_st2 = [Buf(f"st2_{i}") for i in range(2)]
    b_lnst = Buf("lnst")

    p2ps = ExitStack()
    sbank = [ps(f"p2s{i}", [128, 512], F32, p2ps) for i in range(4)]
    b_sbank = [Buf(f"p2s{i}") for i in range(4)]
    atbank = [ps(f"p2a{i}", [128, 1024], BF16, p2ps) for i in range(2)]
    b_atbank = [Buf(f"p2a{i}") for i in range(2)]
    obank = [ps(f"p2o{i}", [128, 512], F32, p2ps) for i in range(2)]
    b_obank = [Buf(f"p2o{i}") for i in range(2)]

    def p2_load_qgroup(g):
        qslot = g % 2
        ch_q[qslot].dma(qblk[qslot][:], q_scr[g, :, :],
                        reads=[b_qscr], writes=[b_qblk[qslot]])

    def p2_loads(qb):
        slot = qb % 2
        q0 = qb * 128
        ch_xr[slot].dma(xres[slot][:], x_d[q0:q0 + 128, :],
                        writes=[b_xres[slot]])

    def p2_s_stats(qb):
        slot = qb % 2
        par = qb % 2
        qslot = (qb // 4) % 2
        qoff = (qb % 4) * 128
        for sc in range(nsc):
            bank = sc % 4
            for dc in range(nkc):
                prog.op("tensor",
                        lambda E_, o=sbank[bank][:],
                        l=qblk[qslot][:, dc * 512 + qoff: dc * 512 + qoff + 128],
                        r=k_sb[:, dc * S + sc * 512: dc * S + sc * 512 + 512],
                        st=(dc == 0), sp=(dc == nkc - 1):
                        E_.matmul(o, l, r, start=st, stop=sp),
                        reads=[b_qblk[qslot], b_ksb], writes=[b_sbank[bank]])
            prog.op("vector",
                    lambda E_, o=stats[par][:, sc:sc + 1], i=sbank[bank][:]:
                    E_.reduce_max(o, i, AX, negate=True),
                    reads=[b_sbank[bank]], writes=[b_stats[par]])
            prog.op("vector",
                    lambda E_, o=s_sb[par][:, sc * 512:(sc + 1) * 512],
                    i=sbank[bank][:]: E_.tensor_copy(o, i),
                    reads=[b_sbank[bank]], writes=[b_ssb[par]])
        prog.op("vector",
                lambda E_, o=st2[par][:, 0:1], i=stats[par][:, 0:nsc]:
                E_.tensor_reduce(o, i, AX, OP.min),
                reads=[b_stats[par]], writes=[b_st2[par]])

    def p2_attend(qb):
        slot = qb % 2
        par = qb % 2
        q0 = qb * 128
        prog.op("scalar",
                lambda E_: E_.activation(a_sb[:], s_sb[par][:], AF.Exp,
                                         bias=st2[par][:, 0:1], scale=1.0),
                reads=[b_ssb[par], b_st2[par]], writes=[b_asb])
        for g in range(natg):
            bank = g % 2
            n_in = min(4, ntc - g * 4)
            for j in range(n_in):
                tc = g * 4 + j
                prog.op("tensor",
                        lambda E_, o=atbank[bank][:, j * 128:(j + 1) * 128],
                        i=a_sb[:, tc * 128:(tc + 1) * 128]:
                        E_.transpose(o, i, idb[:]),
                        reads=[b_asb, b_idb], writes=[b_atbank[bank]])
            prog.op("vector",
                    lambda E_, o=at_sb[:, g * 512: g * 512 + n_in * 128],
                    i=atbank[bank][:, 0:n_in * 128]: E_.tensor_copy(o, i),
                    reads=[b_atbank[bank]], writes=[b_atsb])
        prog.op("vector",
                lambda E_, o=st2[par][:, 1:2]: E_.reduce_sum(o, a_sb[:], AX),
                reads=[b_asb], writes=[b_st2[par]])
        prog.op("vector",
                lambda E_, o=st2[par][:, 2:3], i=st2[par][:, 1:2]:
                E_.reciprocal(o, i),
                reads=[b_st2[par]], writes=[b_st2[par]])
        for (v0, vl) in splits(DK, 512):
            ob = v0 // 512
            for tc in range(ntc):
                prog.op("tensor",
                        lambda E_, o=obank[ob][:, 0:vl],
                        l=at_sb[:, tc * 128:(tc + 1) * 128],
                        r=v_sb[:, tc * DK + v0: tc * DK + v0 + vl],
                        st=(tc == 0), sp=(tc == ntc - 1):
                        E_.matmul(o, l, r, start=st, stop=sp),
                        reads=[b_atsb, b_vsb], writes=[b_obank[ob]])
            prog.op("vector",
                    lambda E_, o=x2pre[:, v0:v0 + vl], i=obank[ob][:, 0:vl],
                    xr=xres[slot][:, v0:v0 + vl], r=st2[par][:, 2:3]:
                    E_.scalar_tensor_tensor(o, i, r, xr, OP.mult, OP.add),
                    reads=[b_obank[ob], b_st2[par], b_xres[slot]],
                    writes=[b_x2pre])

        _emit_layernorm(prog, x2pre[:, 0:D], b_x2pre, sqtmp, b_sqtmp, lnst,
                        b_lnst, x2stage[slot][:, 0:D], b_x2stage[slot], D,
                        epsc, b_epsc)

        ch_x2s[slot].dma(x2_scr[q0:q0 + 128, :], x2stage[slot][:],
                         reads=[b_x2stage[slot]],
                         writes=[b_x2scr[qb // qb_per_tile]])
        ch_x2b[slot].dma(x2b_scr[q0:q0 + 128, :], x2stage[slot][:],
                         reads=[b_x2stage[slot]],
                         writes=[b_x2bscr[qb // qb_per_tile]])

    p2_load_qgroup(0)
    for qb in range(nqb):
        if qb % 4 == 0 and (qb // 4) + 1 <= (nqb - 1) // 4:
            p2_load_qgroup(qb // 4 + 1)
        p2_loads(qb)
        p2_s_stats(qb)
        if qb > 0:
            p2_attend(qb - 1)
    p2_attend(nqb - 1)

    p2ps.close()
    p2.close()
    pA.close()
    prog.barrier(all_chans)

    p3 = ExitStack()
    y_sb = sb("y_sb", [128, nj * D], F32, p3)
    x2fm = [sb(f"x2fm{i}", [128, ndc * TT], BF16, p3) for i in range(2)]
    h_sb = sb("h_sb", [128, nhc * TT], BF16, p3)
    w13_sb = [[sb(f"w1_{i}", [128, ndc * H], BF16, p3),
               sb(f"w3_{i}", [128, ndc * H], BF16, p3)] for i in range(2)]
    w2_sb = sb("w2_sb", [128, nhc * D], BF16, p3)
    t1r = [sb(f"t1r{i}", [128, colw], F32, p3) for i in range(3)]
    lg = sb("lg", [128, E], F32, p3)
    eg = sb("eg", [128, E], F32, p3)
    biased = sb("biased", [128, E], F32, p3)
    rankb = sb("rankb", [128, E], F32, p3)
    gtmp = sb("gtmp", [128, E], F32, p3)
    rw_all = sb("rw_all", [128, nj * E], F32, p3)
    rwp = sb("rwp", [128, EE], BF16, p3)
    rwt = sb("rwt", [EE, TT], BF16, p3)
    gstat = sb("gstat", [128, 8], F32, p3)
    lst = sb("lst", [128, 8], F32, p3)

    b_ysb, b_hsb = Buf("y_sb"), Buf("h_sb")
    b_x2fm = [Buf(f"x2fm{i}") for i in range(2)]
    b_w13 = [[Buf(f"w1_{i}"), Buf(f"w3_{i}")] for i in range(2)]
    b_w2 = Buf("w2_sb")
    b_t1r = [Buf(f"t1r{i}") for i in range(3)]
    b_lg, b_eg, b_biased = Buf("lg"), Buf("eg"), Buf("biased")
    b_rankb, b_gtmp = Buf("rankb"), Buf("gtmp")
    b_rw, b_rwp, b_rwt = Buf("rw_all"), Buf("rwp"), Buf("rwt")
    b_rwp_one = Buf("rwp_one")
    b_gstat = Buf("gstat")
    b_lst = Buf("lst")

    p3ps = ExitStack()
    hps = [ps(f"p3h{i}", [128, 512], F32, p3ps) for i in range(4)]
    b_hps = [Buf(f"p3h{i}") for i in range(4)]
    ops_ = [ps(f"p3o{i}", [128, 512], F32, p3ps) for i in range(2)]
    b_ops = [Buf(f"p3o{i}") for i in range(2)]
    rtp = ps("p3r", [128, 1024], BF16, p3ps)
    b_rtp = Buf("p3r")

    prog.op("vector", lambda E_: E_.memset(rwp[:, E:EE], 1.0), writes=[b_rwp_one])

    def wslot(e, tt):
        return (e + tt) % 2

    def load_w13(e, tt):
        s_ = wslot(e, tt)
        wsel, bsel = w13_sb[s_], b_w13[s_]
        ch = ch_w13[s_]
        ch.dma(wsel[0][:].rearrange("p (c h) -> p c h", c=ndc),
               w1_d[e].rearrange("(c p) h -> p c h", p=128), writes=[bsel[0]])
        ch.dma(wsel[1][:].rearrange("p (c h) -> p c h", c=ndc),
               w3_d[e].rearrange("(c p) h -> p c h", p=128), writes=[bsel[1]])
        tok = ch.batch_token()
        bsel[0].last_w = tok
        bsel[1].last_w = tok

    def load_w2e(e, tt):
        ch_w2[(e + tt) % 2].dma(w2_sb[:].rearrange("p (c d) -> p c d", c=nhc),
                                w2_d[e].rearrange("(c p) d -> p c d", p=128),
                                writes=[b_w2])

    def load_xf(tt):
        xs = tt % 2
        tb = tt * TT
        for dc in range(ndc):
            ch_xf[xs].dma(x2fm[xs][:, dc * TT:(dc + 1) * TT],
                          x2b_scr[tb:tb + TT, dc * 128:(dc + 1) * 128],
                          reads=[b_x2bscr[tt]], writes=[b_x2fm[xs]],
                          transpose=True)

    for tt in range(ntile):
        tb0 = tt * TT
        xslot = tt % 2
        if tt == 0:
            ch_q[0].dma(x2fm[1][:].rearrange("p (j d) -> p j d", j=nj),
                        x2b_scr[0:TT, :].rearrange("(j p) d -> p j d", p=128),
                        reads=[b_x2bscr[0]], writes=[b_x2fm[1]])
            for dc in range(ndc):
                for j in range(nj):
                    prog.op("tensor",
                            lambda E_, o=rtp[:, j * 128:(j + 1) * 128],
                            i=x2fm[1][:, j * D + dc * 128: j * D + (dc + 1) * 128]:
                            E_.transpose(o, i, idb[:]),
                            reads=[b_x2fm[1], b_idb], writes=[b_rtp])
                prog.op("vector",
                        lambda E_, o=x2fm[0][:, dc * TT:(dc + 1) * TT],
                        i=rtp[:, 0:TT]: E_.tensor_copy(o, i),
                        reads=[b_rtp], writes=[b_x2fm[0]])
        ch_y.dma(y_sb[:].rearrange("p (j d) -> p j d", j=nj),
                 x2_scr[tb0:tb0 + TT, :].rearrange("(j p) d -> p j d", p=128),
                 reads=[b_x2scr[tt]], writes=[b_ysb])

        if tt == 0:
            load_w13(0, 0)
            load_w2e(0, 0)

        for j in range(nj):
            gb_ = ops_[j % 2]
            b_gb_ = b_ops[j % 2]
            for dc in range(ndc):
                prog.op("tensor",
                        lambda E_, o=gb_[:, 0:E],
                        l=x2fm[xslot][:, dc * TT + j * 128: dc * TT + (j + 1) * 128],
                        r=gwt[:, dc * E:(dc + 1) * E],
                        st=(dc == 0), sp=(dc == ndc - 1):
                        E_.matmul(o, l, r, start=st, stop=sp),
                        reads=[b_x2fm[xslot], b_gwt], writes=[b_gb_])
            prog.op("vector", lambda E_, i=gb_[:, 0:E], br=b_gb_:
                    E_.tensor_copy(lg[:], i),
                    reads=[b_gb_], writes=[b_lg])
            prog.op("vector",
                    lambda E_: E_.reduce_max(gstat[:, 0:1], lg[:], AX, negate=True),
                    reads=[b_lg], writes=[b_gstat])
            prog.op("scalar",
                    lambda E_: E_.activation(eg[:], lg[:], AF.Exp,
                                             bias=gstat[:, 0:1], scale=1.0),
                    reads=[b_lg, b_gstat], writes=[b_eg])
            prog.op("vector",
                    lambda E_: E_.reduce_sum(gstat[:, 1:2], eg[:], AX),
                    reads=[b_eg], writes=[b_gstat])
            prog.op("vector", lambda E_: E_.reciprocal(gstat[:, 2:3], gstat[:, 1:2]),
                    reads=[b_gstat], writes=[b_gstat])
            sc_ap = rw_all[:, j * E:(j + 1) * E]
            prog.op("vector",
                    lambda E_, o=sc_ap: E_.tensor_scalar(o, eg[:], gstat[:, 2:3],
                                                         None, OP.mult),
                    reads=[b_eg, b_gstat], writes=[b_rw])
            prog.op("vector",
                    lambda E_, o=sc_ap: E_.tensor_tensor(biased[:], o, gbrep[:],
                                                         OP.add),
                    reads=[b_rw, b_gbrep], writes=[b_biased])
            for e in range(E):
                prog.op("vector",
                        lambda E_, e=e: E_.tensor_scalar(gtmp[:], biased[:],
                                                         biased[:, e:e + 1],
                                                         None, OP.is_gt),
                        reads=[b_biased], writes=[b_gtmp])
                prog.op("vector",
                        lambda E_, e=e: E_.reduce_sum(rankb[:, e:e + 1], gtmp[:], AX),
                        reads=[b_gtmp], writes=[b_rankb])
            for e in range(1, E):
                prog.op("vector",
                        lambda E_, e=e: E_.tensor_scalar(gtmp[:, 0:e],
                                                         biased[:, 0:e],
                                                         biased[:, e:e + 1],
                                                         None, OP.is_equal),
                        reads=[b_biased], writes=[b_gtmp])
                prog.op("vector",
                        lambda E_, e=e: E_.reduce_sum(gstat[:, 3:4], gtmp[:, 0:e], AX),
                        reads=[b_gtmp], writes=[b_gstat])
                prog.op("vector",
                        lambda E_, e=e: E_.tensor_tensor(rankb[:, e:e + 1],
                                                         rankb[:, e:e + 1],
                                                         gstat[:, 3:4], OP.add),
                        reads=[b_gstat, b_rankb], writes=[b_rankb])
            prog.op("vector",
                    lambda E_: E_.tensor_scalar(gtmp[:], rankb[:], float(TOPK),
                                                None, OP.is_lt),
                    reads=[b_rankb], writes=[b_gtmp])
            prog.op("vector",
                    lambda E_, o=sc_ap: E_.tensor_tensor(o, o, gtmp[:], OP.mult),
                    reads=[b_gtmp, b_rw], writes=[b_rw])

        if tt + 1 < ntile:
            load_xf(tt + 1)

        for e in range(EE):
            wsel, bsel = w13_sb[wslot(e, tt)], b_w13[wslot(e, tt)]
            if e < E:
                load_w13(e + 1, tt)
            elif tt + 1 < ntile:
                load_w13(0, tt + 1)

            for hc in range(nhc):
                for cc in range(ncol):
                    c0 = cc * colw
                    pb = (hc * ncol + cc) % 2
                    p1b, p3b = hps[2 * pb], hps[2 * pb + 1]
                    bp1, bp3 = b_hps[2 * pb], b_hps[2 * pb + 1]
                    for dc in range(ndc):
                        prog.op("tensor",
                                lambda E_, o=p1b[:, 0:colw],
                                l=wsel[0][:, dc * H + hc * 128: dc * H + (hc + 1) * 128],
                                r=x2fm[xslot][:, dc * TT + c0: dc * TT + c0 + colw],
                                st=(dc == 0), sp=(dc == ndc - 1):
                                E_.matmul(o, l, r, start=st, stop=sp),
                                reads=[bsel[0], b_x2fm[xslot]], writes=[bp1])
                    for dc in range(ndc):
                        prog.op("tensor",
                                lambda E_, o=p3b[:, 0:colw],
                                l=wsel[1][:, dc * H + hc * 128: dc * H + (hc + 1) * 128],
                                r=x2fm[xslot][:, dc * TT + c0: dc * TT + c0 + colw],
                                st=(dc == 0), sp=(dc == ndc - 1):
                                E_.matmul(o, l, r, start=st, stop=sp),
                                reads=[bsel[1], b_x2fm[xslot]], writes=[bp3])
                    tsl = (hc * ncol + cc) % 3
                    prog.op("vector",
                            lambda E_, o=t1r[tsl][:], i=p1b[:, 0:colw],
                            bb=b1c[:, e * nhc + hc: e * nhc + hc + 1]:
                            E_.tensor_scalar(o, i, bb, None, OP.add),
                            reads=[bp1, b_b1c], writes=[b_t1r[tsl]])
                    prog.op("vector",
                            lambda E_, o=t1r[tsl][:], i=p3b[:, 0:colw],
                            t1=t1r[tsl][:],
                            bb=b3c[:, e * nhc + hc: e * nhc + hc + 1]:
                            E_.scalar_tensor_tensor(o, i, bb, t1, OP.add, OP.mult),
                            reads=[bp3, b_b3c, b_t1r[tsl]], writes=[b_t1r[tsl]])
                    prog.op("scalar",
                            lambda E_, o=h_sb[:, hc * TT + c0: hc * TT + c0 + colw],
                            i=t1r[tsl][:]: E_.activation(o, i, AF.Silu),
                            reads=[b_t1r[tsl]], writes=[b_hsb])

            if e == 0:
                for j in range(nj):
                    prog.op("vector",
                            lambda E_, o=rwp[:, 0:E], i=rw_all[:, j * E:(j + 1) * E]:
                            E_.tensor_copy(o, i),
                            reads=[b_rw], writes=[b_rwp])
                    prog.op("tensor",
                            lambda E_, o=rtp[0:EE, j * 128:(j + 1) * 128], i=rwp[:, 0:EE]:
                            E_.transpose(o, i, idb[:]),
                            reads=[b_rwp, b_rwp_one, b_idb], writes=[b_rtp])
                prog.op("vector",
                        lambda E_: E_.tensor_copy(rwt[0:EE, 0:TT], rtp[0:EE, 0:TT]),
                        reads=[b_rtp], writes=[b_rwt])
                for j in range(nj):
                    for (v0, vl) in splits(D, 512):
                        ob = (j * ((D + 511) // 512) + v0 // 512) % 2
                        prog.op("tensor",
                                lambda E_, o=ops_[ob][:, 0:vl],
                                l=rwt[0:EE, j * 128:(j + 1) * 128],
                                r=b2all[0:EE, v0:v0 + vl]:
                                E_.matmul(o, l, r, start=True, stop=True),
                                reads=[b_rwt, b_b2all], writes=[b_ops[ob]])
                        prog.op("vector",
                                lambda E_, o=y_sb[:, j * D + v0: j * D + v0 + vl],
                                i=ops_[ob][:, 0:vl]:
                                E_.tensor_tensor(o, o, i, OP.add),
                                reads=[b_ops[ob], b_ysb], writes=[b_ysb])

            for j in range(nj):
                for (v0, vl) in splits(D, 512):
                    ob = (j * ((D + 511) // 512) + v0 // 512) % 2
                    for hc in range(nhc):
                        prog.op("tensor",
                                lambda E_, o=ops_[ob][:, 0:vl],
                                l=h_sb[:, hc * TT + j * 128: hc * TT + (j + 1) * 128],
                                r=w2_sb[:, hc * D + v0: hc * D + v0 + vl],
                                st=(hc == 0), sp=(hc == nhc - 1):
                                E_.matmul(o, l, r, start=st, stop=sp),
                                reads=[b_hsb, b_w2], writes=[b_ops[ob]])
                    if e < E:
                        prog.op("vector",
                                lambda E_, o=y_sb[:, j * D + v0: j * D + v0 + vl],
                                i=ops_[ob][:, 0:vl],
                                rc=rw_all[:, j * E + e: j * E + e + 1]:
                                E_.scalar_tensor_tensor(o, i, rc, o, OP.mult, OP.add),
                                reads=[b_ops[ob], b_rw, b_ysb], writes=[b_ysb])
                    else:
                        prog.op("vector",
                                lambda E_, o=y_sb[:, j * D + v0: j * D + v0 + vl],
                                i=ops_[ob][:, 0:vl]:
                                E_.tensor_tensor(o, o, i, OP.add),
                                reads=[b_ops[ob], b_ysb], writes=[b_ysb])
                if e == EE - 1:
                    _emit_layernorm(prog, y_sb[:, j * D:(j + 1) * D], b_ysb,
                                    t1r[0], b_t1r[0], lst, b_lst,
                                    y_sb[:, j * D:(j + 1) * D], b_ysb, D,
                                    epsc, b_epsc)
                    ch_out[j % 2].dma(
                        out_d[tb0 + j * 128: tb0 + (j + 1) * 128, :],
                        y_sb[:, j * D:(j + 1) * D], reads=[b_ysb])

            if e < EE - 1:
                load_w2e(e + 1, tt)
            elif tt + 1 < ntile:
                load_w2e(0, tt + 1)

    p3ps.close()

    end = []
    for ch in ch_out:
        if ch.cnt:
            end.append(Tok(ch.sem, ch.key, 16 * ch.cnt))
    prog.finalize(end)
    p3.close()
    return nc, es


def kernel(L0_attn_X, L0_attn_W_q, L0_attn_W_k, L0_attn_W_v,
           L0_attn_b_q, L0_attn_b_k, L0_attn_b_v,
           L0_ffn_gate_w, L0_ffn_gate_b,
           L0_ffn_w1, L0_ffn_b1, L0_ffn_w3, L0_ffn_b3, L0_ffn_w2, L0_ffn_b2,
           L0_ffn_sw1, L0_ffn_sb1, L0_ffn_sw3, L0_ffn_sb3, L0_ffn_sw2, L0_ffn_sb2):
    B, S, D = L0_attn_X.shape
    cfg = dict(S=S, D=D, DK=D, H=int(np.asarray(L0_ffn_w1).shape[2]),
               E=int(np.asarray(L0_ffn_w1).shape[0]),
               TT=1024, scale=1.0 / float(np.sqrt(D)), TOPK=2)

    asnp = lambda a: np.ascontiguousarray(np.asarray(a))
    X = asnp(L0_attn_X)
    bqkv = np.stack([asnp(L0_attn_b_q), asnp(L0_attn_b_k), asnp(L0_attn_b_v)], axis=0)
    w1s = np.concatenate([asnp(L0_ffn_w1), asnp(L0_ffn_sw1)[None]], axis=0)
    w3s = np.concatenate([asnp(L0_ffn_w3), asnp(L0_ffn_sw3)[None]], axis=0)
    w2s = np.concatenate([asnp(L0_ffn_w2), asnp(L0_ffn_sw2)[None]], axis=0)
    b1s = np.concatenate([asnp(L0_ffn_b1), asnp(L0_ffn_sb1)[None]], axis=0)
    b3s = np.concatenate([asnp(L0_ffn_b3), asnp(L0_ffn_sb3)[None]], axis=0)
    b2s = np.concatenate([asnp(L0_ffn_b2), asnp(L0_ffn_sb2)[None]], axis=0)

    shared = {
        "wq": asnp(L0_attn_W_q), "wk": asnp(L0_attn_W_k), "wv": asnp(L0_attn_W_v),
        "bqkv": bqkv, "gw": asnp(L0_ffn_gate_w), "gb": asnp(L0_ffn_gate_b)[None, :],
        "w1s": w1s, "w3s": w3s, "w2s": w2s, "b1s": b1s, "b3s": b3s, "b2s": b2s,
        "ident_bf": np.eye(128, dtype=BF), "ident_f": np.eye(128, dtype=np.float32),
        "ones1": np.ones((1, 128), dtype=np.float32),
    }

    nc, es = build_core_program(cfg)
    in_maps = [dict(shared, x=np.ascontiguousarray(X[c])) for c in range(B)]
    res = run_bass_kernel_spmd(nc, in_maps, core_ids=list(range(B)))
    out = np.stack([np.asarray(res.results[c]["out"]) for c in range(B)], axis=0)
    es.close()
    return out
